# version 10
# baseline (speedup 1.0000x reference)
import math
import numpy as np
from contextlib import ExitStack

import concourse.bass as bass
import concourse.mybir as mybir
from concourse.bass_utils import run_bass_kernel_spmd

F32 = mybir.dt.float32
BF16 = mybir.dt.bfloat16
AF = mybir.ActivationFunctionType
ALU = mybir.AluOpType
AX = mybir.AxisListType

T = 2048
D = 2048
NT = 16
ND = 16
NHEAD = 16
DHEAD = 128
SCALE = np.float32(1.0 / math.sqrt(DHEAD))
EPS = 1e-5
DE = 1408
DS = 2816
NE = 8
NJ_E = DE // 128
NJ_S = DS // 128

STAGES = 4

f32 = np.float32


class Sem:
    def __init__(self, handle, kind):
        self.h = handle
        self.kind = kind
        self.n = 0


class Prog:
    ENGINES = ("tensor", "vector", "scalar", "gpsimd", "sync")

    def __init__(self):
        self.q = {e: [] for e in self.ENGINES}
        self.bw = [None] * 8
        self.br = [None] * 8

    def wait(self, eng, sem, val):
        if val <= 0:
            return
        assert val <= sem.n, f"wait target {val} > emitted count {sem.n}"
        self.q[eng].append(lambda e, s=sem.h, v=val: e.wait_ge(s, v))

    def op(self, eng, meth, *args, inc=None, incn=1, **kw):
        tgt = None
        if inc is not None:
            assert inc.kind == "eng"
            inc.n += incn
            tgt = inc.n

        def f(e, meth=meth, args=args, kw=kw, inc=inc, incn=incn):
            ins = getattr(e, meth)(*args, **kw)
            if inc is not None:
                ins.then_inc(inc.h, incn)

        self.q[eng].append(f)
        return tgt

    def dma(self, eng, dst, src, inc, **kw):
        assert inc.kind == ("swdge" if eng == "gpsimd" else "hwdge")
        inc.n += 16
        tgt = inc.n

        def f(e, dst=dst, src=src, kw=kw, inc=inc):
            e.dma_start(dst, src, **kw).then_inc(inc.h, 16)

        self.q[eng].append(f)
        return tgt

    def ps_pre_write(self, eng, banks):
        for b in banks:
            for rec in (self.br[b], self.bw[b]):
                if rec is not None:
                    self.wait(eng, rec[0], rec[1])

    def ps_post_write(self, banks, sem, cnt):
        for b in banks:
            self.bw[b] = (sem, cnt)
            self.br[b] = None

    def ps_pre_read(self, eng, banks):
        for b in banks:
            if self.bw[b] is not None:
                self.wait(eng, self.bw[b][0], self.bw[b][1])

    def ps_post_read(self, banks, sem, cnt):
        for b in banks:
            if self.br[b] is None or self.br[b][1] < cnt or self.br[b][0] is not sem:
                self.br[b] = (sem, cnt)


def build_nc():
    nc = bass.Bass("TRN2", target_bir_lowering=False)

    inp = {}

    def I(name, shape, dtype=F32):
        inp[name] = nc.dram_tensor(name, shape, dtype, kind="ExternalInput").ap()
        return inp[name]

    x_in = I("x", [T, D])
    for p in "qkvo":
        I(f"basis_{p}", [64, D])
        I(f"phase_{p}", [D, 64])
        I(f"amp_{p}", [D, 64])
    I("Wq", [D, D]); I("bq", [D])
    I("Wk", [512, D]); I("bk", [512])
    I("Wv", [512, D]); I("bv", [512])
    I("Wo", [D, D]); I("bo", [D])
    I("W_router", [D, NE])
    I("We_gate", [NE, D, DE]); I("We_up", [NE, D, DE]); I("We_down", [NE, DE, D])
    I("Ws_gate", [D, DS]); I("Ws_up", [D, DS]); I("Ws_down", [DS, D])
    I("identb", [128, 128], BF16)
    I("identf", [128, 128], F32)
    I("cosq", [128, T], F32)
    I("sinq", [128, T], F32)
    I("maskneg", [128, 128], F32)

    y_out = nc.dram_tensor("y", [T, D], F32, kind="ExternalOutput").ap()

    h_scr = nc.dram_tensor("h_scr", [NE * DE + DS, T], BF16, kind="Internal").ap()
    outT_scr = nc.dram_tensor("outT_scr", [D, T], BF16, kind="Internal").ap()
    wT_scr = nc.dram_tensor("wT_scr", [D, 2 * D + 1024], BF16, kind="Internal").ap()

    P = Prog()

    with ExitStack() as ctx:
        def sb(name, shape, dtype=F32):
            return ctx.enter_context(nc.sbuf_tensor(name, shape, dtype))

        X = sb("X", [128, NT * 2048])
        IDB = sb("IDB", [128, 128], BF16)
        IDF = sb("IDF", [128, 128], F32)
        ONEB = sb("ONEB", [128, 128], BF16)
        ONEF = sb("ONEF", [128, 2], F32)
        MASKN = sb("MASKN", [128, 128], F32)
        SMALL = sb("SMALL", [128, 128], F32)
        COMB = sb("COMB", [128, 128], F32)
        ARENA = sb("ARENA", [128, 19776], F32)

        PS = ctx.enter_context(nc.psum_tensor("PS", [128, 4096], F32))

        def bank(b, n=1):
            return PS[:, 512 * b: 512 * (b + n)]

        def bankb(b, n=1):
            return PS[:, 512 * b: 512 * (b + n)].bitcast(BF16)

        sg = Sem(ctx.enter_context(nc.semaphore("sg")), "swdge")
        sl = Sem(ctx.enter_context(nc.semaphore("sl")), "hwdge")
        st = Sem(ctx.enter_context(nc.semaphore("st")), "hwdge")
        spe = Sem(ctx.enter_context(nc.semaphore("spe")), "eng")
        sve = Sem(ctx.enter_context(nc.semaphore("sve")), "eng")
        sse = Sem(ctx.enter_context(nc.semaphore("sse")), "eng")
        spo = Sem(ctx.enter_context(nc.semaphore("spo")), "eng")

        class Arena:
            def __init__(self, t, n):
                self.t, self.N, self.off, self.marks = t, n, 0, []

            def f(self, n):
                ap = self.t[:, self.off:self.off + n]
                self.off += n
                assert self.off <= self.N, f"arena overflow {self.off}"
                return ap

            def b(self, n):
                return self.f(n // 2).bitcast(BF16)

            def mark(self):
                self.marks.append(self.off)

            def release(self):
                self.off = self.marks.pop()

        A = Arena(ARENA, 19776)

        E, W, DMA = P.op, P.wait, P.dma

        all_sems = []

        def barrier():
            for eng in Prog.ENGINES:
                for s in all_sems:
                    W(eng, s, s.n)
            P.bw = [None] * 8
            P.br = [None] * 8

        for t in range(NT):
            DMA("sync", X[:, 2048 * t:2048 * (t + 1)], x_in[128 * t:128 * (t + 1), :], sl)
        DMA("sync", IDB[:], inp["identb"], sl)
        DMA("sync", IDF[:], inp["identf"], sl)
        DMA("sync", MASKN[:], inp["maskneg"], sl)
        p0_loads = sl.n
        E("vector", "memset", ONEB[:], 1.0, inc=sve)
        ones_ready = E("vector", "memset", ONEF[:], 1.0, inc=sve)

        x_add_done = [0] * NT

        def emit_ln(x_ready_fn, tiles=None, dst_lnT=None, tile_cb=None,
                    stats_save=None, stats_use=None, lnbf_guard=0):
            if tiles is None:
                tiles = list(range(NT))
            A.mark()
            lnbf = [A.b(2048), A.b(2048)]
            if dst_lnT is None:
                lntile = [A.b(2048), A.b(2048)]
                cb_done = []
            else:
                lnT3 = dst_lnT.rearrange("p (d t) -> p d t", d=ND)
            evac_done, apply_done, tr_done = [], [], []
            for i, t in enumerate(tiles):
                Xt = X[:, 2048 * t:2048 * (t + 1)]
                sb8 = 6 * (i % 2)
                c_sum = SMALL[:, sb8:sb8 + 1]
                c_negm = SMALL[:, sb8 + 1:sb8 + 2]
                c_ssq = SMALL[:, sb8 + 2:sb8 + 3]
                c_std = SMALL[:, sb8 + 3:sb8 + 4]
                c_rstd = SMALL[:, sb8 + 4:sb8 + 5]
                c_negmr = SMALL[:, sb8 + 5:sb8 + 6]
                slot = lnbf[i % 2]
                x_ready_fn(t)
                if stats_use is not None:
                    if i >= 2:
                        W("scalar", spe, tr_done[i - 2])
                    else:
                        W("scalar", spe, lnbf_guard)
                    s3 = E("scalar", "activation", slot, Xt, AF.Identity,
                           bias=stats_use[0][:, t:t + 1],
                           scale=stats_use[1][:, t:t + 1], inc=sse)
                    apply_done.append(s3)
                    pb = (0, 1) if i % 2 == 0 else (2, 3)
                    W("tensor", sse, s3)
                    P.ps_pre_write("tensor", pb)
                    psb = bankb(pb[0], 2)
                    for j in range(ND):
                        kw = dict(inc=spe) if j == ND - 1 else {}
                        E("tensor", "transpose", psb[:, 128 * j:128 * (j + 1)],
                          slot[:, 128 * j:128 * (j + 1)], IDB[:], **kw)
                    tr_done.append(spe.n)
                    P.ps_post_write(pb, spe, spe.n)
                    P.ps_pre_read("vector", pb)
                    ve = E("vector", "tensor_copy",
                           lnT3[:, :, 128 * i:128 * (i + 1)],
                           psb.rearrange("p (j c) -> p j c", j=ND), inc=sve)
                    evac_done.append(ve)
                    P.ps_post_read(pb, sve, ve)
                    continue
                if i >= 2:
                    W("vector", sse, apply_done[i - 2])
                v1 = E("vector", "reduce_sum", c_sum, Xt, axis=AX.X, inc=sve)
                W("vector", sve, v1)
                v2 = E("vector", "tensor_scalar", c_negm, c_sum, -1.0 / D, None,
                       op0=ALU.mult, inc=sve)
                W("scalar", sve, v2)
                if i >= 2:
                    W("scalar", spe, tr_done[i - 2])
                else:
                    W("scalar", spe, lnbf_guard)
                s1 = E("scalar", "activation", slot, Xt, AF.Square,
                       bias=c_negm, scale=f32(1.0), accum_out=c_ssq, inc=sse)
                W("scalar", sse, s1)
                s2 = E("scalar", "activation", c_std, c_ssq, AF.Sqrt,
                       bias=f32(EPS), scale=f32(1.0 / D), inc=sse)
                W("vector", sse, s2)
                v3 = E("vector", "reciprocal", c_rstd, c_std, inc=sve)
                W("vector", sve, v3)
                v4 = E("vector", "tensor_mul", c_negmr, c_negm, c_rstd, inc=sve)
                if stats_save is not None:
                    W("vector", sve, v4)
                    E("vector", "tensor_copy", stats_save[0][:, t:t + 1], c_negmr,
                      inc=sve)
                    v4b = E("vector", "tensor_copy", stats_save[1][:, t:t + 1],
                            c_rstd, inc=sve)
                W("scalar", sve, v4)
                s3 = E("scalar", "activation", slot, Xt, AF.Identity,
                       bias=c_negmr, scale=c_rstd, inc=sse)
                apply_done.append(s3)
                pb = (0, 1) if i % 2 == 0 else (2, 3)
                W("tensor", sse, s3)
                P.ps_pre_write("tensor", pb)
                psb = bankb(pb[0], 2)
                for j in range(ND):
                    kw = dict(inc=spe) if j == ND - 1 else {}
                    E("tensor", "transpose", psb[:, 128 * j:128 * (j + 1)],
                      slot[:, 128 * j:128 * (j + 1)], IDB[:], **kw)
                tr_done.append(spe.n)
                P.ps_post_write(pb, spe, spe.n)
                P.ps_pre_read("vector", pb)
                if dst_lnT is not None:
                    ve = E("vector", "tensor_copy",
                           lnT3[:, :, 128 * i:128 * (i + 1)],
                           psb.rearrange("p (j c) -> p j c", j=ND), inc=sve)
                else:
                    if i >= 2:
                        W("vector", spe, cb_done[i - 2])
                    ve = E("vector", "tensor_copy", lntile[i % 2][:], psb, inc=sve)
                evac_done.append(ve)
                P.ps_post_read(pb, sve, ve)
                if dst_lnT is None:
                    cb_done.append(tile_cb(i, t, lntile[i % 2], ve))
            A.release()
            emit_ln.last_tr = tr_done[-1]
            return evac_done[-1]

        def emit_attn_a():
            A.mark()
            basisT = A.b(4 * 1024)
            wTqk = A.b(2048)
            wTvo = A.b(2048)

            A.mark()
            bsb = A.b(2048)
            wstg = A.b(1024)
            pa = A.f(1024)
            am = A.f(1024)
            cs = A.f(1024)
            prev = {}
            W("tensor", sl, p0_loads)
            for pi, p in enumerate("qkvo"):
                if pi > 0:
                    W("gpsimd", spe, prev["btr"])
                    W("gpsimd", sse, prev["sin"])
                    W("gpsimd", sve, prev["wmul"])
                DMA("gpsimd", bsb[0:64, :], inp[f"basis_{p}"], sg)
                DMA("gpsimd", pa.rearrange("p (b h) -> p b h", b=ND), inp[f"phase_{p}"].rearrange("(b p) h -> p b h", p=128), sg)
                g3 = DMA("gpsimd", am.rearrange("p (b h) -> p b h", b=ND), inp[f"amp_{p}"].rearrange("(b p) h -> p b h", p=128), sg)
                W("tensor", sg, g3)
                W("scalar", sg, g3)
                pb = (0,)
                P.ps_pre_write("tensor", pb)
                for j in range(ND):
                    kw = dict(inc=spe) if j == ND - 1 else {}
                    E("tensor", "transpose", bankb(0, 1)[:, 64 * j:64 * (j + 1)],
                      bsb[0:64, 128 * j:128 * (j + 1)], IDB[0:64, 0:64], **kw)
                prev["btr"] = spe.n
                P.ps_post_write(pb, spe, spe.n)
                P.ps_pre_read("vector", pb)
                bev = E("vector", "tensor_copy",
                        basisT[:, 1024 * pi:1024 * (pi + 1)].rearrange("p (j c) -> p j c", j=ND),
                        bankb(0, 1).rearrange("p (j c) -> p j c", j=ND), inc=sve)
                P.ps_post_read(pb, sve, bev)

                if pi > 0:
                    W("scalar", sve, prev["wmul"])
                prev["sin"] = E("scalar", "activation", cs, pa, AF.Sin,
                                bias=f32(math.pi / 2), scale=f32(1.0), inc=sse)
                W("vector", sse, prev["sin"])
                if pi > 0:
                    W("vector", spe, prev["wtr"])
                prev["wmul"] = E("vector", "tensor_mul", wstg, cs, am, inc=sve)
                W("tensor", sve, prev["wmul"])
                pb2 = (2, 3)
                P.ps_pre_write("tensor", pb2)
                for j in range(ND):
                    kw = dict(inc=spe) if j == ND - 1 else {}
                    E("tensor", "transpose", bankb(2, 2)[0:64, 128 * j:128 * (j + 1)],
                      wstg[:, 64 * j:64 * (j + 1)], IDB[:], **kw)
                prev["wtr"] = spe.n
                P.ps_post_write(pb2, spe, spe.n)
                dstw = wTqk if pi < 2 else wTvo
                rows = slice(0, 64) if pi % 2 == 0 else slice(64, 128)
                P.ps_pre_read("vector", pb2)
                wev = E("vector", "tensor_copy",
                        dstw[rows, :].rearrange("p (j c) -> p j c", j=ND),
                        bankb(2, 2)[0:64, :].rearrange("p (j c) -> p j c", j=ND), inc=sve)
                P.ps_post_read(pb2, sve, wev)
            A.release()
            prep_done = sve.n

            resQK = A.b(2048)
            resV = A.b(2048)

            def x_ready(t):
                W("vector", sl, p0_loads)
                W("scalar", sl, p0_loads)

            def res_cb(i, t, tile, evac_cnt):
                last = None
                for pi in range(3):
                    pb = 4 + (pi % 2)
                    W("tensor", sve, evac_cnt)
                    if i == 0 and pi == 0:
                        W("tensor", sve, prep_done)
                    P.ps_pre_write("tensor", (pb,))
                    for d in range(ND):
                        kw = dict(inc=spe) if d == ND - 1 else {}
                        E("tensor", "matmul", bank(pb)[0:64, 0:128],
                          basisT[:, 1024 * pi + 64 * d:1024 * pi + 64 * (d + 1)],
                          tile[:, 128 * d:128 * (d + 1)],
                          start=(d == 0), stop=(d == ND - 1), **kw)
                    last = spe.n
                    P.ps_post_write((pb,), spe, last)
                    dst = [resQK[0:64, :], resQK[64:128, :], resV[0:64, :]][pi]
                    P.ps_pre_read("vector", (pb,))
                    ev = E("vector", "tensor_copy", dst[:, 128 * t:128 * (t + 1)],
                           bank(pb)[0:64, 0:128], inc=sve)
                    P.ps_post_read((pb,), sve, ev)
                return last

            emit_ln(x_ready, tile_cb=res_cb)
            res_done = sve.n

            resOa = A.f(2048)
            qTh = A.b(2048)
            kTh = A.b(2048)
            vh = A.b(2048)
            A.mark()
            probs0_off = A.off
            probs = [A.b(2048), A.b(2048)]
            probsT = A.b(ND * 512)
            outTh = [A.b(512), A.b(512)]
            rz = E("vector", "memset", resOa, 0.0, inc=sve)

            outT_lastread = [spe.n, spe.n]
            probs_T_done = {}
            mx_lastread = {}
            bias_lastread = {}
            resO_last = rz

            for h in range(NHEAD):
                for nm, dst in (("q", qTh), ("k", kTh)):
                    rw = slice(0, 64) if nm == "q" else slice(64, 128)
                    P.ps_pre_write("tensor", (0, 1, 2, 3))
                    W("tensor", sve, res_done)
                    for c in range(4):
                        E("tensor", "matmul", bank(c), wTqk[rw, 128 * h:128 * (h + 1)],
                          resQK[rw, 512 * c:512 * (c + 1)], start=True, stop=True,
                          inc=(spe if c == 3 else None))
                    P.ps_post_write((0, 1, 2, 3), spe, spe.n)
                    P.ps_pre_read("vector", (0, 1, 2, 3))
                    qe = E("vector", "tensor_copy", dst[:], bank(0, 4), inc=sve)
                    P.ps_post_read((0, 1, 2, 3), sve, qe)
                P.ps_pre_write("tensor", (0, 1, 2, 3))
                for tg in range(NT):
                    E("tensor", "matmul", bank(tg // 4)[:, 128 * (tg % 4):128 * (tg % 4 + 1)],
                      resV[0:64, 128 * tg:128 * (tg + 1)], wTvo[0:64, 128 * h:128 * (h + 1)],
                      start=True, stop=True, inc=(spe if tg == NT - 1 else None))
                P.ps_post_write((0, 1, 2, 3), spe, spe.n)
                P.ps_pre_read("vector", (0, 1, 2, 3))
                ve_v = E("vector", "tensor_copy", vh[:], bank(0, 4), inc=sve)
                P.ps_post_read((0, 1, 2, 3), sve, ve_v)
                qkv_done = ve_v

                def head_part(qi):
                    par = qi % 2
                    pslot = probs[par]
                    gqi = 16 * h + qi
                    slot_free = probs_T_done.get(gqi - 2, 0)
                    mx = SMALL[:, 16 + 6 * par:16 + 6 * par + 4]
                    cbias = SMALL[:, 20 + 6 * par:21 + 6 * par]
                    crq = SMALL[:, 21 + 6 * par:22 + 6 * par]
                    for c in range(4):
                        if c == 0:
                            W("tensor", sve, qkv_done)
                        P.ps_pre_write("tensor", (4 + c,))
                        mm = E("tensor", "matmul", bank(4 + c),
                               qTh[:, 128 * qi:128 * (qi + 1)],
                               kTh[:, 512 * c:512 * (c + 1)],
                               start=True, stop=True, inc=spe)
                        P.ps_post_write((4 + c,), spe, mm)
                    W("vector", sve, mx_lastread.get(par, 0))
                    P.ps_pre_read("vector", (4, 5))
                    vm0 = E("vector", "reduce_max", mx[:, 0:1], bank(4, 2),
                            axis=AX.X, inc=sve)
                    P.ps_post_read((4, 5), sve, vm0)
                    P.ps_pre_read("vector", (6, 7))
                    vm1 = E("vector", "reduce_max", mx[:, 1:2], bank(6, 2),
                            axis=AX.X, inc=sve)
                    P.ps_post_read((6, 7), sve, vm1)
                    W("vector", sve, vm1)
                    W("vector", sse, bias_lastread.get(par, 0))
                    vt = E("vector", "tensor_max", cbias, mx[:, 0:1], mx[:, 1:2],
                           inc=sve)
                    W("vector", sve, vt)
                    vbs = E("vector", "tensor_scalar", cbias, cbias, float(-SCALE),
                            None, op0=ALU.mult, inc=sve)
                    return par, pslot, gqi, slot_free, mx, cbias, crq, vbs

                def exp_part(state):
                    par, pslot, gqi, slot_free, mx, cbias, crq, vbs = state
                    last_exp = 0
                    for hf in range(2):
                        W("scalar", sve, vbs)
                        if hf == 0:
                            W("scalar", spe, slot_free)
                        P.ps_pre_read("scalar", (4 + 2 * hf, 5 + 2 * hf))
                        le = E("scalar", "activation",
                               pslot[:, 1024 * hf:1024 * (hf + 1)],
                               bank(4 + 2 * hf, 2), AF.Exp, bias=cbias,
                               scale=SCALE, accum_out=mx[:, 2 + hf:3 + hf], inc=sse)
                        last_exp = le
                        P.ps_post_read((4 + 2 * hf, 5 + 2 * hf), sse, le)
                    bias_lastread[par] = last_exp
                    return last_exp

                def finish_a(state, last_exp):
                    par, pslot, gqi, slot_free, mx, cbias, crq, vbs = state
                    W("vector", sse, last_exp)
                    v1 = E("vector", "tensor_add", crq, mx[:, 2:3], mx[:, 3:4],
                           inc=sve)
                    mx_lastread[par] = v1
                    W("vector", sve, v1)
                    v2 = E("vector", "reciprocal", crq, crq, inc=sve)
                    return v2

                def finish_b(state, v2):
                    par, pslot, gqi, slot_free, mx, cbias, crq, vbs = state
                    W("vector", sve, v2)
                    v3 = E("vector", "tensor_scalar", pslot[:], pslot[:], crq,
                           None, op0=ALU.mult, inc=sve)
                    return v3

                def tail_part(qi, v3):
                    nonlocal resO_last
                    par = qi % 2
                    qq = qi % 4
                    g = qi // 4
                    gqi = 16 * h + qi
                    pslot = probs[par]
                    W("tensor", sve, v3)
                    P.ps_pre_write("tensor", (2, 3))
                    ptb = bankb(2, 2)
                    for j in range(ND):
                        kw = dict(inc=spe) if j == ND - 1 else {}
                        E("tensor", "transpose", ptb[:, 128 * j:128 * (j + 1)],
                          pslot[:, 128 * j:128 * (j + 1)], IDB[:], **kw)
                    probs_T_done[gqi] = spe.n
                    P.ps_post_write((2, 3), spe, spe.n)
                    pT3 = probsT.rearrange("p (k q) -> p k q", k=ND)
                    ptb3 = ptb.rearrange("p (j c) -> p j c", j=ND)
                    P.ps_pre_read("vector", (2,))
                    pev = E("vector", "tensor_copy",
                            pT3[:, 0:8, 128 * qq:128 * (qq + 1)], ptb3[:, 0:8, :],
                            inc=sve)
                    P.ps_post_read((2,), sve, pev)
                    P.ps_pre_read("scalar", (3,))
                    pev2 = E("scalar", "activation",
                             pT3[:, 8:16, 128 * qq:128 * (qq + 1)], ptb3[:, 8:16, :],
                             AF.Identity, bias=f32(0.0), scale=f32(1.0), inc=sse)
                    P.ps_post_read((3,), sse, pev2)
                    if qq == 3:
                        W("tensor", sve, pev)
                        W("tensor", sse, pev2)
                        P.ps_pre_write("tensor", (0,))
                        for k in range(NT):
                            E("tensor", "matmul", bank(0), vh[:, 128 * k:128 * (k + 1)],
                              probsT[:, 512 * k:512 * (k + 1)],
                              start=(k == 0), stop=(k == NT - 1),
                              inc=(spe if k == NT - 1 else None))
                        P.ps_post_write((0,), spe, spe.n)
                        oslot = outTh[g % 2]
                        W("scalar", spe, outT_lastread[g % 2])
                        P.ps_pre_read("scalar", (0,))
                        oev = E("scalar", "activation", oslot[:], bank(0), AF.Identity,
                                bias=f32(0.0), scale=f32(1.0), inc=sse)
                        P.ps_post_read((0,), sse, oev)
                        W("tensor", sse, oev)
                        P.ps_pre_write("tensor", (1,))
                        ro = E("tensor", "matmul", bank(1)[0:64, :],
                               basisT[:, 3 * 1024 + 64 * h:3 * 1024 + 64 * (h + 1)],
                               oslot[:], start=True, stop=True, inc=spe)
                        outT_lastread[g % 2] = ro
                        P.ps_post_write((1,), spe, ro)
                        W("vector", sve, resO_last)
                        P.ps_pre_read("vector", (1,))
                        ra = E("vector", "tensor_add",
                               resOa[0:64, 512 * g:512 * (g + 1)],
                               resOa[0:64, 512 * g:512 * (g + 1)],
                               bank(1)[0:64, :], inc=sve)
                        resO_last = ra
                        P.ps_post_read((1,), sve, ra)

                pending = None
                for qi in range(16):
                    state = head_part(qi)
                    if pending is not None:
                        tail_part(*pending)
                    le = exp_part(state)
                    v2 = finish_a(state, le)
                    v3 = finish_b(state, v2)
                    pending = (qi, v3)
                tail_part(*pending)

            resOb = probs[0]
            W("vector", sve, resO_last)
            W("vector", spe, spe.n)
            rb = E("vector", "tensor_copy", resOb[64:128, 0:2048], resOa[0:64, :], inc=sve)
            for t in range(NT):
                for c in range(4):
                    pb = 2 + ((4 * t + c) % 2)
                    W("tensor", sve, rb)
                    P.ps_pre_write("tensor", (pb,))
                    mm = E("tensor", "matmul", bank(pb),
                           resOb[64:128, 128 * t:128 * (t + 1)],
                           wTvo[64:128, 512 * c:512 * (c + 1)],
                           start=True, stop=True, inc=spe)
                    P.ps_post_write((pb,), spe, mm)
                    W("vector", spe, mm)
                    if x_add_done[t]:
                        W("vector", sve, x_add_done[t])
                    P.ps_pre_read("vector", (pb,))
                    xs = X[:, 2048 * t + 512 * c:2048 * t + 512 * (c + 1)]
                    va = E("vector", "tensor_add", xs, xs, bank(pb), inc=sve)
                    x_add_done[t] = va
                    P.ps_post_read((pb,), sve, va)
            A.release()
            A.release()

        sgw = [Sem(ctx.enter_context(nc.semaphore(f"sgw{i}")), "swdge") for i in range(2)]
        sgu = [Sem(ctx.enter_context(nc.semaphore(f"sgu{i}")), "swdge") for i in range(2)]
        sgd = Sem(ctx.enter_context(nc.semaphore("sgd")), "swdge")
        sgr = Sem(ctx.enter_context(nc.semaphore("sgr")), "swdge")
        sth = [Sem(ctx.enter_context(nc.semaphore(f"sth{i}")), "hwdge") for i in range(2)]
        shr = [Sem(ctx.enter_context(nc.semaphore(f"shr{i}")), "hwdge") for i in range(2)]
        all_sems.extend([sg, sl, st, spe, sve, sse, spo, sgr, sgd] + sgw + sgu + sth + shr)

        def unit_list():
            us = [("s", NE * DE, NJ_S, inp["Ws_gate"], inp["Ws_up"], inp["Ws_down"])]
            for e in range(NE):
                us.append((e, e * DE, NJ_E, inp["We_gate"][e], inp["We_up"][e],
                           inp["We_down"][e]))
            return us

        ffn_state = {"wr_loaded": 0, "comb_lastread": 0, "d_read_done": (0, 0)}

        def emit_ffn(call):
            barrier()
            A.mark()
            RT = A.f(128)
            RTP = A.f(128)
            wrb = A.b(128)
            g0 = DMA("gpsimd", wrb.rearrange("p (d r) -> p d r", d=ND),
                     inp["W_router"].rearrange("(d p) r -> p d r", p=128), sgr)

            units = unit_list()
            gH_done = []
            h_written = []

            for half in range(2):
                if half == 1:
                    barrier()
                tiles = list(range(8 * half, 8 * half + 8))
                A.mark()
                lnT_h = A.b(ND * 1024)

                def x_ready(t):
                    W("vector", sve, x_add_done[t])
                    W("scalar", sve, x_add_done[t])

                ln_done = emit_ln(x_ready, tiles=tiles, dst_lnT=lnT_h)

                W("tensor", sgr, sgr.n)
                rt_ev = None
                A.mark()
                rstage = A.f(1024)
                for c2 in range(2):
                    P.ps_pre_write("tensor", (0,))
                    W("tensor", sve, ln_done)
                    for d in range(ND):
                        E("tensor", "matmul", bank(0)[0:8, :],
                          wrb[:, 8 * d:8 * (d + 1)],
                          lnT_h[:, 1024 * d + 512 * c2:1024 * d + 512 * (c2 + 1)],
                          start=(d == 0), stop=(d == ND - 1),
                          inc=(spe if d == ND - 1 else None))
                    P.ps_post_write((0,), spe, spe.n)
                    P.ps_pre_read("vector", (0,))
                    rev = E("vector", "tensor_copy", rstage[0:8, 512 * c2:512 * (c2 + 1)],
                            bank(0)[0:8, :], inc=sve)
                    P.ps_post_read((0,), sve, rev)
                W("tensor", sve, sve.n)
                W("tensor", sl, p0_loads)
                P.ps_pre_write("tensor", (1,))
                for i8 in range(8):
                    E("tensor", "transpose", bank(1)[:, 8 * i8:8 * (i8 + 1)],
                      rstage[0:8, 128 * i8:128 * (i8 + 1)], IDF[0:8, 0:8],
                      inc=(spe if i8 == 7 else None))
                P.ps_post_write((1,), spe, spe.n)
                P.ps_pre_read("vector", (1,))
                rt_ev = E("vector", "tensor_copy", RT[:, 64 * half:64 * (half + 1)],
                          bank(1)[:, 0:64], inc=sve)
                P.ps_post_read((1,), sve, rt_ev)
                A.release()

                cNM = SMALL[:, 28:29]
                cRS = SMALL[:, 29:30]
                cM1 = SMALL[:, 30:31]
                cM2 = SMALL[:, 31:32]
                tEQ = SMALL[:, 32:40]
                tMK = SMALL[:, 40:48]
                for i, t in enumerate(tiles):
                    rts = RT[:, 8 * (t % 8) + 64 * half:8 * (t % 8) + 64 * half + 8]
                    rtp = RTP[:, 8 * (t % 8) + 64 * half:8 * (t % 8) + 64 * half + 8]
                    W("vector", sve, rt_ev)
                    if i > 0:
                        W("vector", sve, comb_w)
                        W("vector", sse, exp_done)
                    v1 = E("vector", "reduce_max", cNM, rts, axis=AX.X,
                           negate=True, inc=sve)
                    W("scalar", sve, v1)
                    exp_done = E("scalar", "activation", rtp, rts, AF.Exp,
                                 bias=cNM, scale=f32(1.0), accum_out=cRS, inc=sse)
                    W("vector", sse, exp_done)
                    v2 = E("vector", "reciprocal", cRS, cRS, inc=sve)
                    W("vector", sve, v2)
                    v3 = E("vector", "tensor_scalar", rtp, rtp, cRS, None,
                           op0=ALU.mult, inc=sve)
                    W("vector", sve, v3)
                    v4 = E("vector", "reduce_max", cM1, rtp, axis=AX.X, inc=sve)
                    W("vector", sve, v4)
                    v5 = E("vector", "tensor_scalar", tEQ, rtp, cM1, None,
                           op0=ALU.is_equal, inc=sve)
                    W("vector", sve, v5)
                    v6 = E("vector", "scalar_tensor_tensor", tMK, tEQ, -4.0, rtp,
                           op0=ALU.mult, op1=ALU.add, inc=sve)
                    W("vector", sve, v6)
                    v7 = E("vector", "reduce_max", cM2, tMK, axis=AX.X, inc=sve)
                    W("vector", sve, v7)
                    v8 = E("vector", "tensor_scalar", tEQ, rtp, cM2, None,
                           op0=ALU.is_ge, inc=sve)
                    W("vector", sve, v8)
                    W("vector", sve, ffn_state["comb_lastread"])
                    comb_w = E("vector", "tensor_mul", COMB[:, 8 * t:8 * (t + 1)],
                               tEQ, rtp, inc=sve)
                comb_done = sve.n

                wgb = [A.b(2048), A.b(2048)]
                wub = [A.b(2048), A.b(2048)]
                sgl = [A.f(1024), A.f(1024)]
                hjb = [A.b(1024), A.b(1024)]
                mm_done = {}
                silu_done = {}
                hmul_done = {}
                hdma_done = {}
                jj = 0
                W("gpsimd", sse, sse.n)
                W("gpsimd", sve, sve.n)
                W("gpsimd", spe, spe.n)
                if call > 0:
                    W("sync", shr[0], ffn_state["d_read_done"][0])
                    W("sync", shr[1], ffn_state["d_read_done"][1])
                for (uname, row0, nj, gsrc, usrc, dsrc) in units:
                    for j in range(nj):
                        sp = jj % 2
                        W("gpsimd", spe, mm_done.get(("g", sp), 0))
                        for dh in range(2):
                            DMA("gpsimd",
                                wgb[sp].rearrange("p (d j) -> p d j", d=ND)[:, 8 * dh:8 * (dh + 1), :],
                                gsrc[1024 * dh:1024 * (dh + 1), 128 * j:128 * (j + 1)].rearrange("(d p) j -> p d j", p=128),
                                sgw[sp])
                        W("gpsimd", spe, mm_done.get(("u", sp), 0))
                        for dh in range(2):
                            DMA("gpsimd",
                                wub[sp].rearrange("p (d j) -> p d j", d=ND)[:, 8 * dh:8 * (dh + 1), :],
                                usrc[1024 * dh:1024 * (dh + 1), 128 * j:128 * (j + 1)].rearrange("(d p) j -> p d j", p=128),
                                sgu[sp])
                        gb = (0, 1) if sp == 0 else (4, 5)
                        ub = (2, 3) if sp == 0 else (6, 7)
                        W("tensor", sgw[sp], sgw[sp].n)
                        W("tensor", sve, ln_done)
                        P.ps_pre_write("tensor", gb)
                        for c2 in range(2):
                            for d in range(ND):
                                E("tensor", "matmul", bank(gb[c2]),
                                  wgb[sp][:, 128 * d:128 * (d + 1)],
                                  lnT_h[:, 1024 * d + 512 * c2:1024 * d + 512 * (c2 + 1)],
                                  start=(d == 0), stop=(d == ND - 1),
                                  inc=(spe if (c2 == 1 and d == ND - 1) else None))
                        mm_done[("g", sp)] = spe.n
                        P.ps_post_write(gb, spe, spe.n)
                        W("tensor", sgu[sp], sgu[sp].n)
                        P.ps_pre_write("tensor", ub)
                        for c2 in range(2):
                            for d in range(ND):
                                E("tensor", "matmul", bank(ub[c2]),
                                  wub[sp][:, 128 * d:128 * (d + 1)],
                                  lnT_h[:, 1024 * d + 512 * c2:1024 * d + 512 * (c2 + 1)],
                                  start=(d == 0), stop=(d == ND - 1),
                                  inc=(spe if (c2 == 1 and d == ND - 1) else None))
                        mm_done[("u", sp)] = spe.n
                        P.ps_post_write(ub, spe, spe.n)
                        W("scalar", sve, hmul_done.get(sp, 0))
                        P.ps_pre_read("scalar", gb)
                        sd = E("scalar", "activation", sgl[sp], bank(gb[0], 2),
                               AF.Silu, bias=f32(0.0), scale=f32(1.0), inc=sse)
                        silu_done[sp] = sd
                        P.ps_post_read(gb, sse, sd)
                        W("vector", sse, sd)
                        W("vector", sth[sp], hdma_done.get(sp, 0))
                        P.ps_pre_read("vector", ub)
                        hm = E("vector", "tensor_mul", hjb[sp], sgl[sp],
                               bank(ub[0], 2), inc=sve)
                        hmul_done[sp] = hm
                        P.ps_post_read(ub, sve, hm)
                        W("sync", sve, hm)
                        hdma_done[sp] = DMA("sync",
                                            h_scr[row0 + 128 * j:row0 + 128 * (j + 1),
                                                  1024 * half:1024 * (half + 1)],
                                            hjb[sp], sth[sp])
                        jj += 1
                gH_done.append(spe.n)
                h_written.append((sth[0].n, sth[1].n))
                A.release()

            def emit_sub(uname, row0, nj, dsrc, oh, ow, wd_ld, sp):
                shared = uname == "s"
                nb = ow // 512
                for ti, t in enumerate(range(NT)):
                    hsp = ti % 2
                    W("sync", spe, hb_used.get(hsp, 0))
                    rb = DMA("sync",
                             hb_e[hsp].rearrange("p (j c) -> p j c", j=NJ_S)[:, 0:nj, :],
                             h_scr[row0:row0 + nj * 128, 128 * t:128 * (t + 1)].rearrange("(j p) c -> p j c", p=128),
                             shr[hsp])
                    ybs = ((0, 1) if ti % 2 == 0 else (4, 5))[:nb]
                    W("tensor", shr[hsp], rb)
                    W("tensor", sgdp[sp], wd_ld)
                    P.ps_pre_write("tensor", ybs)
                    for j in range(nj):
                        for cc in range(nb):
                            E("tensor", "matmul", bank(ybs[cc]),
                              hb_e[hsp][:, 128 * j:128 * (j + 1)],
                              wd_sb[sp][:, ow * j + 512 * cc:ow * j + 512 * (cc + 1)],
                              start=(j == 0), stop=(j == nj - 1),
                              skip_group_check=True,
                              inc=(spe if (j == nj - 1 and cc == nb - 1) else None))
                    hb_used[hsp] = spe.n
                    P.ps_post_write(ybs, spe, spe.n)
                    W("vector", spe, spe.n)
                    W("vector", sve, x_add_done[t])
                    P.ps_pre_read("vector", ybs)
                    xsl = X[:, 2048 * t + ow * oh:2048 * t + ow * oh + ow]
                    if shared:
                        va = E("vector", "tensor_add", xsl, xsl, bank(ybs[0], nb),
                               inc=sve)
                    else:
                        va = E("vector", "scalar_tensor_tensor", xsl,
                               bank(ybs[0], nb), COMB[:, 8 * t + uname:8 * t + uname + 1],
                               xsl, op0=ALU.mult, op1=ALU.add, inc=sve)
                    x_add_done[t] = va
                    P.ps_post_read(ybs, sve, va)
                wd_used[sp] = spe.n

            barrier()
            A.mark()
            wd_sb = [A.b(NJ_E * 1024), A.b(NJ_E * 1024)]
            hb_e = [A.b(NJ_S * 128), A.b(NJ_S * 128)]
            W("sync", sth[0], sth[0].n)
            W("sync", sth[1], sth[1].n)
            wd_used = {}
            hb_used = {}
            sgdp = [sgw[0], sgw[1]]
            subunits = []
            for (uname, row0, nj, gsrc, usrc, dsrc) in units:
                ow = 512 if uname == "s" else 1024
                for oh in range(2048 // ow):
                    subunits.append((uname, row0, nj, dsrc, oh, ow))
            for si, su in enumerate(subunits):
                uname, row0, nj, dsrc, oh, ow = su
                sp = si % 2
                W("gpsimd", spe, wd_used.get(sp, 0))
                for j in range(nj):
                    DMA("gpsimd", wd_sb[sp][:, ow * j:ow * (j + 1)],
                        dsrc[128 * j:128 * (j + 1), ow * oh:ow * (oh + 1)],
                        sgdp[sp])
                subunits[si] = (uname, row0, nj, dsrc, oh, ow, sgdp[sp].n, sp)
                if si >= 1:
                    emit_sub(*subunits[si - 1])
            emit_sub(*subunits[-1])
            ffn_state["comb_lastread"] = sve.n
            ffn_state["d_read_done"] = (shr[0].n, shr[1].n)
            A.release()
            A.release()

        swq = [Sem(ctx.enter_context(nc.semaphore(f"swq{i}")), "hwdge") for i in range(2)]
        swk = Sem(ctx.enter_context(nc.semaphore("swk")), "hwdge")
        stw = [Sem(ctx.enter_context(nc.semaphore(f"stw{i}")), "hwdge") for i in range(2)]
        sgt = [Sem(ctx.enter_context(nc.semaphore(f"sgt{i}")), "swdge") for i in range(2)]
        sot = [Sem(ctx.enter_context(nc.semaphore(f"sot{i}")), "hwdge") for i in range(2)]
        all_sems.extend(swq + [swk] + stw + sgt + sot)

        def emit_wtprep():
            barrier()
            A.mark()
            wtmp = [A.b(2048), A.b(2048)]
            wstage = [A.b(2048), A.b(2048)]
            blocks = []
            for ob in range(16):
                blocks.append((inp["Wq"], ob, 128 * ob))
            for ob in range(4):
                blocks.append((inp["Wk"], ob, 2048 + 128 * ob))
            for ob in range(4):
                blocks.append((inp["Wv"], ob, 2560 + 128 * ob))
            for ob in range(16):
                blocks.append((inp["Wo"], ob, 3072 + 128 * ob))
            tr_done = {}
            ev_done = {}
            dma_out = {}
            for i, (src, ob, colb) in enumerate(blocks):
                sp = i % 2
                if i >= 2:
                    W("gpsimd", spe, tr_done[i - 2])
                DMA("gpsimd", wtmp[sp], src[128 * ob:128 * (ob + 1), :], sgt[sp])
                W("tensor", sgt[sp], sgt[sp].n)
                pb = (0, 1) if sp == 0 else (2, 3)
                P.ps_pre_write("tensor", pb)
                psb = bankb(pb[0], 2)
                for j in range(ND):
                    kw = dict(inc=spe) if j == ND - 1 else {}
                    E("tensor", "transpose", psb[:, 128 * j:128 * (j + 1)],
                      wtmp[sp][:, 128 * j:128 * (j + 1)], IDB[:], **kw)
                tr_done[i] = spe.n
                P.ps_post_write(pb, spe, spe.n)
                if i >= 2:
                    W("vector", stw[sp], dma_out[i - 2])
                P.ps_pre_read("vector", pb)
                ev = E("vector", "tensor_copy", wstage[sp][:], psb, inc=sve)
                ev_done[i] = ev
                P.ps_post_read(pb, sve, ev)
                W("sync", sve, ev)
                dma_out[i] = DMA("sync",
                                 wT_scr[:, colb:colb + 128].rearrange("(j p) c -> p j c", p=128),
                                 wstage[sp].rearrange("p (j c) -> p j c", j=ND), stw[sp])
            bq_tm = SMALL[:, 80:96]
            bk_tm = SMALL[:, 96:100]
            bv_tm = SMALL[:, 100:104]
            bq_sw = SMALL[:, 12:28]
            bk_sw = SMALL[:, 116:120]
            for hh in range(16):
                DMA("sync", bq_tm[:, hh:hh + 1], inp["bq"][128 * hh:128 * (hh + 1)].unsqueeze(1), sl)
                DMA("sync", bq_sw[0:64, hh:hh + 1], inp["bq"][128 * hh + 64:128 * (hh + 1)].unsqueeze(1), sl)
                DMA("sync", bq_sw[64:128, hh:hh + 1], inp["bq"][128 * hh:128 * hh + 64].unsqueeze(1), sl)
            for gg in range(4):
                DMA("sync", bk_tm[:, gg:gg + 1], inp["bk"][128 * gg:128 * (gg + 1)].unsqueeze(1), sl)
                DMA("sync", bk_sw[0:64, gg:gg + 1], inp["bk"][128 * gg + 64:128 * (gg + 1)].unsqueeze(1), sl)
                DMA("sync", bk_sw[64:128, gg:gg + 1], inp["bk"][128 * gg:128 * gg + 64].unsqueeze(1), sl)
                DMA("sync", bv_tm[:, gg:gg + 1], inp["bv"][128 * gg:128 * (gg + 1)].unsqueeze(1), sl)
            A.release()
            return sl.n

        def emit_attn_b():
            bias_loaded = emit_wtprep()
            negmr_sv = SMALL[:, 48:64]
            rstd_sv = SMALL[:, 64:80]
            bq_tm = SMALL[:, 80:96]
            bk_tm = SMALL[:, 96:100]
            bv_tm = SMALL[:, 100:104]
            bq_sw = SMALL[:, 12:28]
            bk_sw = SMALL[:, 116:120]
            cMx = SMALL[:, 104:108]
            cMxA = SMALL[:, 108:112]
            cMxB = SMALL[:, 112:116]

            for pss in range(4):
                barrier()
                A.mark()
                qT = A.b(4 * 2048)
                kT = A.b(2048)
                vg = A.b(2048)
                A.mark()
                lnchunk = A.b(ND * 512)
                wqb = [A.b(2048), A.b(2048)]
                wkv = A.b(4096)
                tblc = A.f(512)
                tbls = A.f(512)
                tmp1 = A.f(512)
                tmp2 = A.f(512)
                g = pss

                ln_ev = {}
                rope_done = 0
                tbl_used = 0
                wkv_used = 0
                wq_used = {}
                v_ev = 0
                for c in range(4):
                    tiles = list(range(4 * c, 4 * c + 4))

                    def x_ready(t):
                        W("vector", sve, x_add_done[t])
                        W("scalar", sve, x_add_done[t])

                    lg = getattr(emit_ln, "last_tr", 0) if c > 0 else 0
                    if pss == 0:
                        lnd = emit_ln(x_ready, tiles=tiles, dst_lnT=lnchunk,
                                      stats_save=(negmr_sv, rstd_sv), lnbf_guard=lg)
                    else:
                        lnd = emit_ln(x_ready, tiles=tiles, dst_lnT=lnchunk,
                                      stats_use=(negmr_sv, rstd_sv), lnbf_guard=lg)
                    W("sync", sve, rope_done)
                    DMA("sync", tblc, inp["cosq"][:, 512 * c:512 * (c + 1)], sl)
                    DMA("sync", tbls, inp["sinq"][:, 512 * c:512 * (c + 1)], sl)
                    W("sync", spe, wkv_used)
                    DMA("sync", wkv[:, 0:2048].rearrange("p (d j) -> p d j", d=ND)[:, 0:8, :],
                        wT_scr[0:1024, 2048 + 128 * g:2048 + 128 * (g + 1)].rearrange("(d p) c -> p d c", p=128), swk)
                    DMA("sync", wkv[:, 0:2048].rearrange("p (d j) -> p d j", d=ND)[:, 8:16, :],
                        wT_scr[1024:2048, 2048 + 128 * g:2048 + 128 * (g + 1)].rearrange("(d p) c -> p d c", p=128), swk)
                    DMA("sync", wkv[:, 2048:4096].rearrange("p (d j) -> p d j", d=ND)[:, 0:8, :],
                        wT_scr[0:1024, 2560 + 128 * g:2560 + 128 * (g + 1)].rearrange("(d p) c -> p d c", p=128), swk)
                    kvld = DMA("sync", wkv[:, 2048:4096].rearrange("p (d j) -> p d j", d=ND)[:, 8:16, :],
                               wT_scr[1024:2048, 2560 + 128 * g:2560 + 128 * (g + 1)].rearrange("(d p) c -> p d c", p=128), swk)

                    def rope_evac(pbk, bias_col, bias_sw, dst):
                        nonlocal rope_done
                        W("vector", sl, sl.n)
                        W("vector", sl, bias_loaded)
                        W("vector", sve, rope_done)
                        P.ps_pre_read("vector", (pbk,))
                        r1 = E("vector", "scalar_tensor_tensor", tmp1, bank(pbk),
                               bias_col, tblc, op0=ALU.add, op1=ALU.mult, inc=sve)
                        r2 = E("vector", "scalar_tensor_tensor", tmp2[0:64, :],
                               bank(pbk)[64:128, :], bias_sw[0:64, :], tbls[0:64, :],
                               op0=ALU.add, op1=ALU.mult, inc=sve)
                        r3 = E("vector", "scalar_tensor_tensor", tmp2[64:128, :],
                               bank(pbk)[0:64, :], bias_sw[64:128, :], tbls[64:128, :],
                               op0=ALU.add, op1=ALU.mult, inc=sve)
                        W("vector", sve, r3)
                        r4 = E("vector", "tensor_add", dst, tmp1, tmp2, inc=sve)
                        rope_done = r4
                        P.ps_post_read((pbk,), sve, r4)
                        return r4

                    W("tensor", swk, kvld)
                    W("tensor", sve, lnd)
                    pb = 4
                    P.ps_pre_write("tensor", (pb,))
                    for d in range(ND):
                        E("tensor", "matmul", bank(pb),
                          wkv[:, 128 * d:128 * (d + 1)],
                          lnchunk[:, 512 * d:512 * (d + 1)],
                          start=(d == 0), stop=(d == ND - 1),
                          inc=(spe if d == ND - 1 else None))
                    P.ps_post_write((pb,), spe, spe.n)
                    rope_evac(pb, bk_tm[:, g:g + 1], bk_sw[:, g:g + 1], kT[:, 512 * c:512 * (c + 1)])
                    for hl in range(4):
                        h = 4 * pss + hl
                        sp = hl % 2
                        W("sync", spe, wq_used.get(sp, 0))
                        DMA("sync", wqb[sp].rearrange("p (d j) -> p d j", d=ND)[:, 0:8, :],
                            wT_scr[0:1024, 128 * h:128 * (h + 1)].rearrange("(d p) c -> p d c", p=128), swq[sp])
                        qld = DMA("sync", wqb[sp].rearrange("p (d j) -> p d j", d=ND)[:, 8:16, :],
                                  wT_scr[1024:2048, 128 * h:128 * (h + 1)].rearrange("(d p) c -> p d c", p=128), swq[sp])
                        pbq = 5 + sp
                        W("tensor", swq[sp], qld)
                        W("tensor", sve, lnd)
                        P.ps_pre_write("tensor", (pbq,))
                        for d in range(ND):
                            E("tensor", "matmul", bank(pbq),
                              wqb[sp][:, 128 * d:128 * (d + 1)],
                              lnchunk[:, 512 * d:512 * (d + 1)],
                              start=(d == 0), stop=(d == ND - 1),
                              inc=(spe if d == ND - 1 else None))
                        wq_used[sp] = spe.n
                        P.ps_post_write((pbq,), spe, spe.n)
                        rope_evac(pbq, bq_tm[:, h:h + 1], bq_sw[:, h:h + 1],
                                  qT[:, 2048 * hl + 512 * c:2048 * hl + 512 * (c + 1)])
                    W("tensor", swk, kvld)
                    P.ps_pre_write("tensor", (7,))
                    for tt in range(4):
                        for d in range(ND):
                            E("tensor", "matmul", bank(7)[:, 128 * tt:128 * (tt + 1)],
                              lnchunk[:, 512 * d + 128 * tt:512 * d + 128 * (tt + 1)],
                              wkv[:, 2048 + 128 * d:2048 + 128 * (d + 1)],
                              start=(d == 0), stop=(d == ND - 1),
                              inc=(spe if (tt == 3 and d == ND - 1) else None))
                    wkv_used = spe.n
                    P.ps_post_write((7,), spe, spe.n)
                    P.ps_pre_read("vector", (7,))
                    v_ev = E("vector", "tensor_copy", vg[:, 512 * c:512 * (c + 1)],
                             bank(7), inc=sve)
                    P.ps_post_read((7,), sve, v_ev)
                qkv_done = sve.n
                A.release()
                barrier()

                A.mark()
                probs = [A.b(2048), A.b(2048)]
                probsT = A.b(ND * 512)
                outTh = [A.b(512), A.b(512)]
                probs_T_done = {}
                mx_lastread = {}
                bias_lastread = {}
                rq_lastread = {}
                outT_dma = {}
                pz = 0
                def b_head_part(hl, qi):
                    qTh_ = qT[:, 2048 * hl:2048 * (hl + 1)]
                    par = qi % 2
                    pslot = probs[par]
                    gqi = 64 * pss + 16 * hl + qi
                    nvalid = qi + 1
                    wlast = (qi % 4 + 1) * 128
                    nch = qi // 4 + 1
                    slot_free = probs_T_done.get(gqi - 2, 0)
                    mxb = cMxA if par == 0 else cMxB
                    dm = 0
                    for kc in range(nch):
                        wd = 512 if kc < nch - 1 else wlast
                        if kc == 0:
                            W("tensor", sve, qkv_done)
                        P.ps_pre_write("tensor", (4 + kc,))
                        mm = E("tensor", "matmul", bank(4 + kc)[:, 0:wd],
                               qTh_[:, 128 * qi:128 * (qi + 1)],
                               kT[:, 512 * kc:512 * kc + wd],
                               start=True, stop=True, inc=spe)
                        P.ps_post_write((4 + kc,), spe, mm)
                        if kc == nch - 1:
                            W("vector", spe, mm)
                            W("vector", sl, p0_loads)
                            P.ps_pre_read("vector", (4 + kc,))
                            dm = E("vector", "tensor_add",
                                   bank(4 + kc)[:, wd - 128:wd],
                                   bank(4 + kc)[:, wd - 128:wd], MASKN[:], inc=sve)
                        if kc == 0:
                            W("vector", sve, mx_lastread.get(par, 0))
                        if kc == nch - 1:
                            W("vector", sve, dm)
                        P.ps_pre_read("vector", (4 + kc,))
                        vm = E("vector", "reduce_max", mxb[:, kc:kc + 1],
                               bank(4 + kc)[:, 0:wd], axis=AX.X, inc=sve)
                        P.ps_post_read((4 + kc,), sve, vm)
                    W("vector", sve, sve.n)
                    W("vector", sse, bias_lastread.get(par, 0))
                    vt = E("vector", "reduce_max", cMx[:, 2 + par:3 + par],
                           mxb[:, 0:nch], axis=AX.X, inc=sve)
                    W("vector", sve, vt)
                    vb = E("vector", "tensor_scalar", cMx[:, 2 + par:3 + par],
                           cMx[:, 2 + par:3 + par], float(-SCALE), None,
                           op0=ALU.mult, inc=sve)
                    return (hl, qi, par, pslot, gqi, slot_free, mxb, nch, wlast,
                            nvalid, vb)

                def b_exp_part(stt_):
                    hl, qi, par, pslot, gqi, slot_free, mxb, nch, wlast, nvalid, vb = stt_
                    last_exp = 0
                    for kc in range(nch):
                        wd = 512 if kc < nch - 1 else wlast
                        W("scalar", sve, vb)
                        if kc == 0:
                            W("scalar", spe, slot_free)
                        le = E("scalar", "activation",
                               pslot[:, 512 * kc:512 * kc + wd],
                               bank(4 + kc)[:, 0:wd], AF.Exp,
                               bias=cMx[:, 2 + par:3 + par], scale=SCALE,
                               accum_out=mxb[:, kc:kc + 1], inc=sse)
                        last_exp = le
                        P.ps_post_read((4 + kc,), sse, le)
                    bias_lastread[par] = last_exp
                    return last_exp

                def b_finish_a(stt_, last_exp):
                    hl, qi, par, pslot, gqi, slot_free, mxb, nch, wlast, nvalid, vb = stt_
                    W("vector", sse, last_exp)
                    v1 = E("vector", "reduce_sum", cMx[:, par:par + 1],
                           mxb[:, 0:nch], axis=AX.X, inc=sve)
                    mx_lastread[par] = v1
                    W("vector", sve, v1)
                    v2 = E("vector", "reciprocal", cMx[:, par:par + 1],
                           cMx[:, par:par + 1], inc=sve)
                    return v2

                def b_finish_b(stt_, v2):
                    hl, qi, par, pslot, gqi, slot_free, mxb, nch, wlast, nvalid, vb = stt_
                    W("vector", sve, v2)
                    v3 = E("vector", "tensor_scalar",
                           pslot[:, 0:128 * nvalid], pslot[:, 0:128 * nvalid],
                           cMx[:, par:par + 1], None, op0=ALU.mult, inc=sve)
                    return v3

                def b_tail_part(stt_, v3):
                    hl, qi, par, pslot, gqi, slot_free, mxb, nch, wlast, nvalid, vb = stt_
                    h = 4 * pss + hl
                    qq = qi % 4
                    grp = qi // 4
                    W("tensor", sve, v3)
                    P.ps_pre_write("tensor", (2, 3))
                    ptb = bankb(2, 2)
                    nt_this = min(nvalid, ND)
                    for j in range(nt_this):
                        kw = dict(inc=spe) if j == nt_this - 1 else {}
                        E("tensor", "transpose", ptb[:, 128 * j:128 * (j + 1)],
                          pslot[:, 128 * j:128 * (j + 1)], IDB[:], **kw)
                    probs_T_done[gqi] = spe.n
                    P.ps_post_write((2, 3), spe, spe.n)
                    pT3 = probsT.rearrange("p (k q) -> p k q", k=ND)
                    ptb3 = ptb.rearrange("p (j c) -> p j c", j=ND)
                    nv_ = min(nt_this, 8)
                    pev2 = 0
                    P.ps_pre_read("vector", (2,))
                    pev = E("vector", "tensor_copy",
                            pT3[:, 0:nv_, 128 * qq:128 * (qq + 1)],
                            ptb3[:, 0:nv_, :], inc=sve)
                    P.ps_post_read((2,), sve, pev)
                    if nt_this > 8:
                        P.ps_pre_read("scalar", (3,))
                        pev2 = E("scalar", "activation",
                                 pT3[:, 8:nt_this, 128 * qq:128 * (qq + 1)],
                                 ptb3[:, 8:nt_this, :],
                                 AF.Identity, bias=f32(0.0), scale=f32(1.0), inc=sse)
                        P.ps_post_read((3,), sse, pev2)
                    for kb in range(4 * grp + qq + 1, 4 * grp + 4):
                        E("vector", "memset",
                          pT3[:, kb:kb + 1, 128 * qq:128 * (qq + 1)],
                          0.0, inc=sve)
                    if qq == 3:
                        W("tensor", sve, sve.n)
                        if pev2:
                            W("tensor", sse, pev2)
                        nkb = 4 * grp + 4
                        P.ps_pre_write("tensor", (0,))
                        for k in range(nkb):
                            E("tensor", "matmul", bank(0), vg[:, 128 * k:128 * (k + 1)],
                              probsT[:, 512 * k:512 * (k + 1)],
                              start=(k == 0), stop=(k == nkb - 1),
                              inc=(spe if k == nkb - 1 else None))
                        P.ps_post_write((0,), spe, spe.n)
                        oslot = outTh[grp % 2]
                        W("scalar", sot[grp % 2], outT_dma.get(grp % 2, 0))
                        W("scalar", sl, bias_loaded)
                        P.ps_pre_read("scalar", (0,))
                        oev = E("scalar", "activation", oslot[:], bank(0),
                                AF.Identity, bias=bv_tm[:, g:g + 1],
                                scale=f32(1.0), inc=sse)
                        P.ps_post_read((0,), sse, oev)
                        W("sync", sse, oev)
                        outT_dma[grp % 2] = DMA("sync",
                                                outT_scr[128 * h:128 * (h + 1),
                                                         512 * grp:512 * (grp + 1)],
                                                oslot[:], sot[grp % 2])

                pending = None
                for hl in range(4):
                    for qi in range(16):
                        stt_ = b_head_part(hl, qi)
                        if pending is not None:
                            b_tail_part(*pending)
                        le = b_exp_part(stt_)
                        v2 = b_finish_a(stt_, le)
                        v3 = b_finish_b(stt_, v2)
                        pending = (stt_, v3)
                b_tail_part(*pending)
                A.release()
                A.release()

            barrier()
            A.mark()
            woT = A.b(ND * 2048)
            obuf = [A.b(2048), A.b(2048)]
            bo_row = A.b(2048)
            for d in range(ND):
                DMA("sync", woT[:, 2048 * d:2048 * (d + 1)],
                    wT_scr[128 * d:128 * (d + 1), 3072:5120], swk)
            DMA("gpsimd", bo_row[0:1, :], inp["bo"].unsqueeze(0), sg)
            wo_ld = swk.n
            bo_ld = sg.n
            ob_used = {}
            for t in range(NT):
                sp = t % 2
                W("sync", spe, ob_used.get(sp, 0))
                orb = DMA("sync", obuf[sp].rearrange("p (j c) -> p j c", j=ND),
                          outT_scr[:, 128 * t:128 * (t + 1)].rearrange("(j p) c -> p j c", p=128),
                          sot[sp])
                ybs = (0, 1, 2, 3) if sp == 0 else (4, 5, 6, 7)
                W("tensor", sot[sp], orb)
                W("tensor", swk, wo_ld)
                W("tensor", sg, bo_ld)
                P.ps_pre_write("tensor", ybs)
                for d in range(ND):
                    for cc in range(4):
                        E("tensor", "matmul", bank(ybs[cc]),
                          obuf[sp][:, 128 * d:128 * (d + 1)],
                          woT[:, 2048 * d + 512 * cc:2048 * d + 512 * (cc + 1)],
                          start=(d == 0), stop=False, skip_group_check=True)
                for cc in range(4):
                    E("tensor", "matmul", bank(ybs[cc]), ONEB[0:1, :],
                      bo_row[0:1, 512 * cc:512 * (cc + 1)],
                      start=False, stop=True, skip_group_check=True,
                      inc=(spe if cc == 3 else None))
                ob_used[sp] = spe.n
                P.ps_post_write(ybs, spe, spe.n)
                W("vector", spe, spe.n)
                W("vector", sve, x_add_done[t])
                P.ps_pre_read("vector", ybs)
                xsl = X[:, 2048 * t:2048 * (t + 1)]
                va = E("vector", "tensor_add", xsl, xsl, bank(ybs[0], 4), inc=sve)
                x_add_done[t] = va
                P.ps_post_read(ybs, sve, va)
            A.release()

        if STAGES >= 1:
            emit_attn_a()
        if STAGES >= 2:
            emit_ffn(0)
        if STAGES >= 3:
            emit_attn_b()
        if STAGES >= 4:
            emit_ffn(1)

        for t in range(NT):
            W("sync", sve, x_add_done[t] if x_add_done[t] else sve.n)
            DMA("sync", y_out[128 * t:128 * (t + 1), :], X[:, 2048 * t:2048 * (t + 1)], st)
        W("sync", st, st.n)
        W("sync", sl, sl.n)
        W("sync", sg, sg.n)

        block = ctx.enter_context(nc.Block())
        for eng in Prog.ENGINES:
            fns = P.q[eng]
            getattr(block, eng)(lambda e, fns=fns: [f(e) for f in fns])

    return nc


def make_constants():
    ident = np.eye(128, dtype=np.float32)
    import ml_dtypes
    identb = ident.astype(ml_dtypes.bfloat16)
    i0 = np.arange(64, dtype=np.float64)
    thetas = 10000.0 ** (-2.0 * i0 / 128.0)
    pos = np.arange(1, T + 1, dtype=np.float64)
    ang = pos[None, :] * thetas[:, None]
    cosq = np.concatenate([np.cos(ang), np.cos(ang)], axis=0).astype(np.float32)
    sinq = np.concatenate([-np.sin(ang), np.sin(ang)], axis=0).astype(np.float32)
    q = np.arange(128)[:, None]
    k = np.arange(128)[None, :]
    maskneg = np.where(k > q, -1e9, 0.0).astype(np.float32)
    return dict(identb=identb, identf=ident, cosq=cosq, sinq=sinq, maskneg=maskneg)


def make_in_maps(inputs):
    consts = make_constants()
    x = np.ascontiguousarray(inputs["L0_attn_a_hidden_states"], dtype=np.float32)
    wmap = dict(
        Wq=inputs["L0_attn_b_Wq"], bq=inputs["L0_attn_b_bq"],
        Wk=inputs["L0_attn_b_Wk"], bk=inputs["L0_attn_b_bk"],
        Wv=inputs["L0_attn_b_Wv"], bv=inputs["L0_attn_b_bv"],
        Wo=inputs["L0_attn_b_Wo"], bo=inputs["L0_attn_b_bo"],
        W_router=inputs["L0_ffn_W_router"],
        We_gate=inputs["L0_ffn_We_gate"], We_up=inputs["L0_ffn_We_up"],
        We_down=inputs["L0_ffn_We_down"],
        Ws_gate=inputs["L0_ffn_Ws_gate"], Ws_up=inputs["L0_ffn_Ws_up"],
        Ws_down=inputs["L0_ffn_Ws_down"],
    )
    for p in "qkvo":
        wmap[f"basis_{p}"] = inputs[f"L0_attn_a_basis_{p}"]
        wmap[f"phase_{p}"] = inputs[f"L0_attn_a_phase_{p}"]
        wmap[f"amp_{p}"] = inputs[f"L0_attn_a_amp_{p}"]
    wmap = {k: np.ascontiguousarray(v, dtype=np.float32) for k, v in wmap.items()}
    return [dict(x=np.ascontiguousarray(x[c]), **wmap, **consts)
            for c in range(x.shape[0])]


def kernel(**inputs):
    nc = build_nc()
    in_maps = make_in_maps(inputs)
    res = run_bass_kernel_spmd(nc, in_maps, core_ids=list(range(8)))
    return np.stack([np.asarray(r["y"], dtype=np.float32) for r in res.results], axis=0)


# version 11
# speedup vs baseline: 1.0124x; 1.0124x over previous
import math
import numpy as np
from contextlib import ExitStack

import concourse.bass as bass
import concourse.mybir as mybir
from concourse.bass_utils import run_bass_kernel_spmd

F32 = mybir.dt.float32
BF16 = mybir.dt.bfloat16
AF = mybir.ActivationFunctionType
ALU = mybir.AluOpType
AX = mybir.AxisListType

T = 2048
D = 2048
NT = 16
ND = 16
NHEAD = 16
DHEAD = 128
SCALE = np.float32(1.0 / math.sqrt(DHEAD))
EPS = 1e-5
DE = 1408
DS = 2816
NE = 8
NJ_E = DE // 128
NJ_S = DS // 128

STAGES = 4

f32 = np.float32


class Sem:
    def __init__(self, handle, kind):
        self.h = handle
        self.kind = kind
        self.n = 0


class Prog:
    ENGINES = ("tensor", "vector", "scalar", "gpsimd", "sync")

    def __init__(self):
        self.q = {e: [] for e in self.ENGINES}
        self.bw = [None] * 8
        self.br = [None] * 8

    def wait(self, eng, sem, val):
        if val <= 0:
            return
        assert val <= sem.n, f"wait target {val} > emitted count {sem.n}"
        self.q[eng].append(lambda e, s=sem.h, v=val: e.wait_ge(s, v))

    def op(self, eng, meth, *args, inc=None, incn=1, **kw):
        tgt = None
        if inc is not None:
            assert inc.kind == "eng"
            inc.n += incn
            tgt = inc.n

        def f(e, meth=meth, args=args, kw=kw, inc=inc, incn=incn):
            ins = getattr(e, meth)(*args, **kw)
            if inc is not None:
                ins.then_inc(inc.h, incn)

        self.q[eng].append(f)
        return tgt

    def dma(self, eng, dst, src, inc, **kw):
        assert inc.kind == ("swdge" if eng == "gpsimd" else "hwdge")
        inc.n += 16
        tgt = inc.n

        def f(e, dst=dst, src=src, kw=kw, inc=inc):
            e.dma_start(dst, src, **kw).then_inc(inc.h, 16)

        self.q[eng].append(f)
        return tgt

    def ps_pre_write(self, eng, banks):
        for b in banks:
            for rec in (self.br[b], self.bw[b]):
                if rec is not None:
                    self.wait(eng, rec[0], rec[1])

    def ps_post_write(self, banks, sem, cnt):
        for b in banks:
            self.bw[b] = (sem, cnt)
            self.br[b] = None

    def ps_pre_read(self, eng, banks):
        for b in banks:
            if self.bw[b] is not None:
                self.wait(eng, self.bw[b][0], self.bw[b][1])

    def ps_post_read(self, banks, sem, cnt):
        for b in banks:
            if self.br[b] is None or self.br[b][1] < cnt or self.br[b][0] is not sem:
                self.br[b] = (sem, cnt)


def build_nc():
    nc = bass.Bass("TRN2", target_bir_lowering=False)

    inp = {}

    def I(name, shape, dtype=F32):
        inp[name] = nc.dram_tensor(name, shape, dtype, kind="ExternalInput").ap()
        return inp[name]

    x_in = I("x", [T, D])
    for p in "qkvo":
        I(f"basis_{p}", [64, D])
        I(f"phase_{p}", [D, 64])
        I(f"amp_{p}", [D, 64])
    I("Wq", [D, D]); I("bq", [D])
    I("Wk", [512, D]); I("bk", [512])
    I("Wv", [512, D]); I("bv", [512])
    I("Wo", [D, D]); I("bo", [D])
    I("W_router", [D, NE])
    I("We_gate", [NE, D, DE]); I("We_up", [NE, D, DE]); I("We_down", [NE, DE, D])
    I("Ws_gate", [D, DS]); I("Ws_up", [D, DS]); I("Ws_down", [DS, D])
    I("identb", [128, 128], BF16)
    I("identf", [128, 128], F32)
    I("cosq", [128, T], F32)
    I("sinq", [128, T], F32)
    I("maskneg", [128, 128], F32)

    y_out = nc.dram_tensor("y", [T, D], F32, kind="ExternalOutput").ap()

    h_scr = nc.dram_tensor("h_scr", [NE * DE + DS, T], BF16, kind="Internal").ap()
    outT_scr = nc.dram_tensor("outT_scr", [D, T], BF16, kind="Internal").ap()
    wT_scr = nc.dram_tensor("wT_scr", [D, 2 * D + 1024], BF16, kind="Internal").ap()

    P = Prog()

    with ExitStack() as ctx:
        def sb(name, shape, dtype=F32):
            return ctx.enter_context(nc.sbuf_tensor(name, shape, dtype))

        X = sb("X", [128, NT * 2048])
        IDB = sb("IDB", [128, 128], BF16)
        IDF = sb("IDF", [128, 128], F32)
        ONEB = sb("ONEB", [128, 128], BF16)
        ONEF = sb("ONEF", [128, 2], F32)
        MASKN = sb("MASKN", [128, 128], F32)
        SMALL = sb("SMALL", [128, 128], F32)
        COMB = sb("COMB", [128, 128], F32)
        ARENA = sb("ARENA", [128, 19776], F32)

        PS = ctx.enter_context(nc.psum_tensor("PS", [128, 4096], F32))

        def bank(b, n=1):
            return PS[:, 512 * b: 512 * (b + n)]

        def bankb(b, n=1):
            return PS[:, 512 * b: 512 * (b + n)].bitcast(BF16)

        sg = Sem(ctx.enter_context(nc.semaphore("sg")), "swdge")
        sl = Sem(ctx.enter_context(nc.semaphore("sl")), "hwdge")
        st = Sem(ctx.enter_context(nc.semaphore("st")), "hwdge")
        spe = Sem(ctx.enter_context(nc.semaphore("spe")), "eng")
        sve = Sem(ctx.enter_context(nc.semaphore("sve")), "eng")
        sse = Sem(ctx.enter_context(nc.semaphore("sse")), "eng")
        spo = Sem(ctx.enter_context(nc.semaphore("spo")), "eng")

        class Arena:
            def __init__(self, t, n):
                self.t, self.N, self.off, self.marks = t, n, 0, []

            def f(self, n):
                ap = self.t[:, self.off:self.off + n]
                self.off += n
                assert self.off <= self.N, f"arena overflow {self.off}"
                return ap

            def b(self, n):
                return self.f(n // 2).bitcast(BF16)

            def mark(self):
                self.marks.append(self.off)

            def release(self):
                self.off = self.marks.pop()

        A = Arena(ARENA, 19776)

        E, W, DMA = P.op, P.wait, P.dma

        all_sems = []

        def barrier():
            for eng in Prog.ENGINES:
                for s in all_sems:
                    W(eng, s, s.n)
            P.bw = [None] * 8
            P.br = [None] * 8

        for t in range(NT):
            DMA("sync", X[:, 2048 * t:2048 * (t + 1)], x_in[128 * t:128 * (t + 1), :], sl)
        DMA("sync", IDB[:], inp["identb"], sl)
        DMA("sync", IDF[:], inp["identf"], sl)
        DMA("sync", MASKN[:], inp["maskneg"], sl)
        p0_loads = sl.n
        E("vector", "memset", ONEB[:], 1.0, inc=sve)
        ones_ready = E("vector", "memset", ONEF[:], 1.0, inc=sve)

        x_add_done = [0] * NT

        def emit_ln(x_ready_fn, tiles=None, dst_lnT=None, tile_cb=None,
                    stats_save=None, stats_use=None, lnbf_guard=0):
            if tiles is None:
                tiles = list(range(NT))
            A.mark()
            lnbf = [A.b(2048), A.b(2048)]
            if dst_lnT is None:
                lntile = [A.b(2048), A.b(2048)]
                cb_done = []
            else:
                lnT3 = dst_lnT.rearrange("p (d t) -> p d t", d=ND)
            evac_done, apply_done, tr_done = [], [], []
            for i, t in enumerate(tiles):
                Xt = X[:, 2048 * t:2048 * (t + 1)]
                sb8 = 6 * (i % 2)
                c_sum = SMALL[:, sb8:sb8 + 1]
                c_negm = SMALL[:, sb8 + 1:sb8 + 2]
                c_ssq = SMALL[:, sb8 + 2:sb8 + 3]
                c_std = SMALL[:, sb8 + 3:sb8 + 4]
                c_rstd = SMALL[:, sb8 + 4:sb8 + 5]
                c_negmr = SMALL[:, sb8 + 5:sb8 + 6]
                slot = lnbf[i % 2]
                x_ready_fn(t)
                if stats_use is not None:
                    if i >= 2:
                        W("scalar", spe, tr_done[i - 2])
                    else:
                        W("scalar", spe, lnbf_guard)
                    s3 = E("scalar", "activation", slot, Xt, AF.Identity,
                           bias=stats_use[0][:, t:t + 1],
                           scale=stats_use[1][:, t:t + 1], inc=sse)
                    apply_done.append(s3)
                    pb = (0, 1) if i % 2 == 0 else (2, 3)
                    W("tensor", sse, s3)
                    P.ps_pre_write("tensor", pb)
                    psb = bankb(pb[0], 2)
                    for j in range(ND):
                        kw = dict(inc=spe) if j == ND - 1 else {}
                        E("tensor", "transpose", psb[:, 128 * j:128 * (j + 1)],
                          slot[:, 128 * j:128 * (j + 1)], IDB[:], **kw)
                    tr_done.append(spe.n)
                    P.ps_post_write(pb, spe, spe.n)
                    P.ps_pre_read("vector", pb)
                    ve = E("vector", "tensor_copy",
                           lnT3[:, :, 128 * i:128 * (i + 1)],
                           psb.rearrange("p (j c) -> p j c", j=ND), inc=sve)
                    evac_done.append(ve)
                    P.ps_post_read(pb, sve, ve)
                    continue
                if i >= 2:
                    W("vector", sse, apply_done[i - 2])
                v1 = E("vector", "reduce_sum", c_sum, Xt, axis=AX.X, inc=sve)
                W("vector", sve, v1)
                v2 = E("vector", "tensor_scalar", c_negm, c_sum, -1.0 / D, None,
                       op0=ALU.mult, inc=sve)
                W("scalar", sve, v2)
                if i >= 2:
                    W("scalar", spe, tr_done[i - 2])
                else:
                    W("scalar", spe, lnbf_guard)
                s1 = E("scalar", "activation", slot, Xt, AF.Square,
                       bias=c_negm, scale=f32(1.0), accum_out=c_ssq, inc=sse)
                W("scalar", sse, s1)
                s2 = E("scalar", "activation", c_std, c_ssq, AF.Sqrt,
                       bias=f32(EPS), scale=f32(1.0 / D), inc=sse)
                W("vector", sse, s2)
                v3 = E("vector", "reciprocal", c_rstd, c_std, inc=sve)
                W("vector", sve, v3)
                v4 = E("vector", "tensor_mul", c_negmr, c_negm, c_rstd, inc=sve)
                if stats_save is not None:
                    W("vector", sve, v4)
                    E("vector", "tensor_copy", stats_save[0][:, t:t + 1], c_negmr,
                      inc=sve)
                    v4b = E("vector", "tensor_copy", stats_save[1][:, t:t + 1],
                            c_rstd, inc=sve)
                W("scalar", sve, v4)
                s3 = E("scalar", "activation", slot, Xt, AF.Identity,
                       bias=c_negmr, scale=c_rstd, inc=sse)
                apply_done.append(s3)
                pb = (0, 1) if i % 2 == 0 else (2, 3)
                W("tensor", sse, s3)
                P.ps_pre_write("tensor", pb)
                psb = bankb(pb[0], 2)
                for j in range(ND):
                    kw = dict(inc=spe) if j == ND - 1 else {}
                    E("tensor", "transpose", psb[:, 128 * j:128 * (j + 1)],
                      slot[:, 128 * j:128 * (j + 1)], IDB[:], **kw)
                tr_done.append(spe.n)
                P.ps_post_write(pb, spe, spe.n)
                P.ps_pre_read("vector", pb)
                if dst_lnT is not None:
                    ve = E("vector", "tensor_copy",
                           lnT3[:, :, 128 * i:128 * (i + 1)],
                           psb.rearrange("p (j c) -> p j c", j=ND), inc=sve)
                else:
                    if i >= 2:
                        W("vector", spe, cb_done[i - 2])
                    ve = E("vector", "tensor_copy", lntile[i % 2][:], psb, inc=sve)
                evac_done.append(ve)
                P.ps_post_read(pb, sve, ve)
                if dst_lnT is None:
                    cb_done.append(tile_cb(i, t, lntile[i % 2], ve))
            A.release()
            emit_ln.last_tr = tr_done[-1]
            return evac_done[-1]

        def emit_attn_a():
            A.mark()
            basisT = A.b(4 * 1024)
            wTqk = A.b(2048)
            wTvo = A.b(2048)

            A.mark()
            bsb = A.b(2048)
            wstg = A.b(1024)
            pa = A.f(1024)
            am = A.f(1024)
            cs = A.f(1024)
            prev = {}
            W("tensor", sl, p0_loads)
            for pi, p in enumerate("qkvo"):
                if pi > 0:
                    W("gpsimd", spe, prev["btr"])
                    W("gpsimd", sse, prev["sin"])
                    W("gpsimd", sve, prev["wmul"])
                DMA("gpsimd", bsb[0:64, :], inp[f"basis_{p}"], sg)
                DMA("gpsimd", pa.rearrange("p (b h) -> p b h", b=ND), inp[f"phase_{p}"].rearrange("(b p) h -> p b h", p=128), sg)
                g3 = DMA("gpsimd", am.rearrange("p (b h) -> p b h", b=ND), inp[f"amp_{p}"].rearrange("(b p) h -> p b h", p=128), sg)
                W("tensor", sg, g3)
                W("scalar", sg, g3)
                pb = (0,)
                P.ps_pre_write("tensor", pb)
                for j in range(ND):
                    kw = dict(inc=spe) if j == ND - 1 else {}
                    E("tensor", "transpose", bankb(0, 1)[:, 64 * j:64 * (j + 1)],
                      bsb[0:64, 128 * j:128 * (j + 1)], IDB[0:64, 0:64], **kw)
                prev["btr"] = spe.n
                P.ps_post_write(pb, spe, spe.n)
                P.ps_pre_read("vector", pb)
                bev = E("vector", "tensor_copy",
                        basisT[:, 1024 * pi:1024 * (pi + 1)].rearrange("p (j c) -> p j c", j=ND),
                        bankb(0, 1).rearrange("p (j c) -> p j c", j=ND), inc=sve)
                P.ps_post_read(pb, sve, bev)

                if pi > 0:
                    W("scalar", sve, prev["wmul"])
                prev["sin"] = E("scalar", "activation", cs, pa, AF.Sin,
                                bias=f32(math.pi / 2), scale=f32(1.0), inc=sse)
                W("vector", sse, prev["sin"])
                if pi > 0:
                    W("vector", spe, prev["wtr"])
                prev["wmul"] = E("vector", "tensor_mul", wstg, cs, am, inc=sve)
                W("tensor", sve, prev["wmul"])
                pb2 = (2, 3)
                P.ps_pre_write("tensor", pb2)
                for j in range(ND):
                    kw = dict(inc=spe) if j == ND - 1 else {}
                    E("tensor", "transpose", bankb(2, 2)[0:64, 128 * j:128 * (j + 1)],
                      wstg[:, 64 * j:64 * (j + 1)], IDB[:], **kw)
                prev["wtr"] = spe.n
                P.ps_post_write(pb2, spe, spe.n)
                dstw = wTqk if pi < 2 else wTvo
                rows = slice(0, 64) if pi % 2 == 0 else slice(64, 128)
                P.ps_pre_read("vector", pb2)
                wev = E("vector", "tensor_copy",
                        dstw[rows, :].rearrange("p (j c) -> p j c", j=ND),
                        bankb(2, 2)[0:64, :].rearrange("p (j c) -> p j c", j=ND), inc=sve)
                P.ps_post_read(pb2, sve, wev)
            A.release()
            prep_done = sve.n

            resQK = A.b(2048)
            resV = A.b(2048)

            def x_ready(t):
                W("vector", sl, p0_loads)
                W("scalar", sl, p0_loads)

            def res_cb(i, t, tile, evac_cnt):
                last = None
                for pi in range(3):
                    pb = 4 + (pi % 2)
                    W("tensor", sve, evac_cnt)
                    if i == 0 and pi == 0:
                        W("tensor", sve, prep_done)
                    P.ps_pre_write("tensor", (pb,))
                    for d in range(ND):
                        kw = dict(inc=spe) if d == ND - 1 else {}
                        E("tensor", "matmul", bank(pb)[0:64, 0:128],
                          basisT[:, 1024 * pi + 64 * d:1024 * pi + 64 * (d + 1)],
                          tile[:, 128 * d:128 * (d + 1)],
                          start=(d == 0), stop=(d == ND - 1), **kw)
                    last = spe.n
                    P.ps_post_write((pb,), spe, last)
                    dst = [resQK[0:64, :], resQK[64:128, :], resV[0:64, :]][pi]
                    P.ps_pre_read("vector", (pb,))
                    ev = E("vector", "tensor_copy", dst[:, 128 * t:128 * (t + 1)],
                           bank(pb)[0:64, 0:128], inc=sve)
                    P.ps_post_read((pb,), sve, ev)
                return last

            emit_ln(x_ready, tile_cb=res_cb)
            res_done = sve.n

            resOa = A.f(2048)
            qTh = A.b(2048)
            kTh = A.b(2048)
            vh = A.b(2048)
            A.mark()
            probs0_off = A.off
            probs = [A.b(2048), A.b(2048)]
            probsT = A.b(ND * 512)
            outTh = [A.b(512), A.b(512)]
            rz = E("vector", "memset", resOa, 0.0, inc=sve)

            outT_lastread = [spe.n, spe.n]
            probs_T_done = {}
            mx_lastread = {}
            bias_lastread = {}
            resO_last = rz

            for h in range(NHEAD):
                for nm, dst in (("q", qTh), ("k", kTh)):
                    rw = slice(0, 64) if nm == "q" else slice(64, 128)
                    P.ps_pre_write("tensor", (0, 1, 2, 3))
                    W("tensor", sve, res_done)
                    for c in range(4):
                        E("tensor", "matmul", bank(c), wTqk[rw, 128 * h:128 * (h + 1)],
                          resQK[rw, 512 * c:512 * (c + 1)], start=True, stop=True,
                          inc=(spe if c == 3 else None))
                    P.ps_post_write((0, 1, 2, 3), spe, spe.n)
                    P.ps_pre_read("vector", (0, 1, 2, 3))
                    qe = E("vector", "tensor_copy", dst[:], bank(0, 4), inc=sve)
                    P.ps_post_read((0, 1, 2, 3), sve, qe)
                P.ps_pre_write("tensor", (0, 1, 2, 3))
                for tg in range(NT):
                    E("tensor", "matmul", bank(tg // 4)[:, 128 * (tg % 4):128 * (tg % 4 + 1)],
                      resV[0:64, 128 * tg:128 * (tg + 1)], wTvo[0:64, 128 * h:128 * (h + 1)],
                      start=True, stop=True, inc=(spe if tg == NT - 1 else None))
                P.ps_post_write((0, 1, 2, 3), spe, spe.n)
                P.ps_pre_read("vector", (0, 1, 2, 3))
                ve_v = E("vector", "tensor_copy", vh[:], bank(0, 4), inc=sve)
                P.ps_post_read((0, 1, 2, 3), sve, ve_v)
                qkv_done = ve_v

                def head_part(qi):
                    par = qi % 2
                    pslot = probs[par]
                    gqi = 16 * h + qi
                    slot_free = probs_T_done.get(gqi - 2, 0)
                    cb = 16 + 8 * par
                    mx = SMALL[:, cb:cb + 2]
                    cbias = SMALL[:, cb + 2:cb + 4]
                    csum = SMALL[:, cb + 4:cb + 6]
                    cfac = SMALL[:, cb + 6:cb + 8]
                    mx_free = mx_lastread.get(par, 0)
                    bias_free = bias_lastread.get(par, 0)
                    last_exp = 0
                    for hf in range(2):
                        pb = (4 + 2 * hf, 5 + 2 * hf)
                        for c2 in range(2):
                            c = 2 * hf + c2
                            if c == 0:
                                W("tensor", sve, qkv_done)
                            P.ps_pre_write("tensor", (pb[c2],))
                            mm = E("tensor", "matmul", bank(pb[c2]),
                                   qTh[:, 128 * qi:128 * (qi + 1)],
                                   kTh[:, 512 * c:512 * (c + 1)],
                                   start=True, stop=True, inc=spe)
                            P.ps_post_write((pb[c2],), spe, mm)
                        if hf == 0:
                            W("vector", sve, mx_free)
                        P.ps_pre_read("vector", pb)
                        vm = E("vector", "reduce_max", mx[:, hf:hf + 1],
                               bank(pb[0], 2), axis=AX.X, inc=sve)
                        P.ps_post_read(pb, sve, vm)
                        W("vector", sve, vm)
                        if hf == 0:
                            W("vector", sse, bias_free)
                        vb = E("vector", "tensor_scalar", cbias[:, hf:hf + 1],
                               mx[:, hf:hf + 1], float(-SCALE), None,
                               op0=ALU.mult, inc=sve)
                        W("scalar", sve, vb)
                        if hf == 0:
                            W("scalar", spe, slot_free)
                        le = E("scalar", "activation",
                               pslot[:, 1024 * hf:1024 * (hf + 1)],
                               bank(pb[0], 2), AF.Exp, bias=cbias[:, hf:hf + 1],
                               scale=SCALE, accum_out=csum[:, hf:hf + 1], inc=sse)
                        last_exp = le
                        P.ps_post_read(pb, sse, le)
                    bias_lastread[par] = last_exp
                    return par, pslot, gqi, mx, cbias, csum, cfac, last_exp

                def finish_part(state):
                    par, pslot, gqi, mx, cbias, csum, cfac, last_exp = state
                    W("vector", sse, last_exp)
                    v1 = E("vector", "tensor_max", cbias[:, 0:1], mx[:, 0:1],
                           mx[:, 1:2], inc=sve)
                    W("vector", sve, v1)
                    v2 = E("vector", "tensor_scalar", cbias[:, 0:1], cbias[:, 0:1],
                           -1.0, None, op0=ALU.mult, inc=sve)
                    W("vector", sve, v2)
                    v3 = E("vector", "tensor_scalar", cfac, mx,
                           cbias[:, 0:1], float(SCALE), op0=ALU.add,
                           op1=ALU.mult, inc=sve)
                    W("scalar", sve, v3)
                    s1 = E("scalar", "activation", cfac, cfac, AF.Exp,
                           bias=f32(0.0), scale=f32(1.0), inc=sse)
                    W("vector", sse, s1)
                    v4 = E("vector", "tensor_mul", csum, csum, cfac, inc=sve)
                    W("vector", sve, v4)
                    v5 = E("vector", "tensor_add", csum[:, 0:1], csum[:, 0:1],
                           csum[:, 1:2], inc=sve)
                    mx_lastread[par] = v5
                    W("vector", sve, v5)
                    v6 = E("vector", "reciprocal", csum[:, 0:1], csum[:, 0:1],
                           inc=sve)
                    W("vector", sve, v6)
                    v7 = E("vector", "tensor_scalar", cfac, cfac, csum[:, 0:1],
                           None, op0=ALU.mult, inc=sve)
                    W("vector", sve, v7)
                    v8 = E("vector", "tensor_scalar",
                           pslot[:, 0:1024], pslot[:, 0:1024],
                           cfac[:, 0:1], None, op0=ALU.mult, inc=sve)
                    W("vector", sve, v8)
                    v9 = E("vector", "tensor_scalar",
                           pslot[:, 1024:2048], pslot[:, 1024:2048],
                           cfac[:, 1:2], None, op0=ALU.mult, inc=sve)
                    return v9

                def tail_part(qi, v3):
                    nonlocal resO_last
                    par = qi % 2
                    qq = qi % 4
                    g = qi // 4
                    gqi = 16 * h + qi
                    pslot = probs[par]
                    W("tensor", sve, v3)
                    P.ps_pre_write("tensor", (2, 3))
                    ptb = bankb(2, 2)
                    for j in range(ND):
                        kw = dict(inc=spe) if j == ND - 1 else {}
                        E("tensor", "transpose", ptb[:, 128 * j:128 * (j + 1)],
                          pslot[:, 128 * j:128 * (j + 1)], IDB[:], **kw)
                    probs_T_done[gqi] = spe.n
                    P.ps_post_write((2, 3), spe, spe.n)
                    pT3 = probsT.rearrange("p (k q) -> p k q", k=ND)
                    ptb3 = ptb.rearrange("p (j c) -> p j c", j=ND)
                    P.ps_pre_read("vector", (2,))
                    pev = E("vector", "tensor_copy",
                            pT3[:, 0:8, 128 * qq:128 * (qq + 1)], ptb3[:, 0:8, :],
                            inc=sve)
                    P.ps_post_read((2,), sve, pev)
                    P.ps_pre_read("scalar", (3,))
                    pev2 = E("scalar", "activation",
                             pT3[:, 8:16, 128 * qq:128 * (qq + 1)], ptb3[:, 8:16, :],
                             AF.Identity, bias=f32(0.0), scale=f32(1.0), inc=sse)
                    P.ps_post_read((3,), sse, pev2)
                    if qq == 3:
                        W("tensor", sve, pev)
                        W("tensor", sse, pev2)
                        P.ps_pre_write("tensor", (0,))
                        for k in range(NT):
                            E("tensor", "matmul", bank(0), vh[:, 128 * k:128 * (k + 1)],
                              probsT[:, 512 * k:512 * (k + 1)],
                              start=(k == 0), stop=(k == NT - 1),
                              inc=(spe if k == NT - 1 else None))
                        P.ps_post_write((0,), spe, spe.n)
                        oslot = outTh[g % 2]
                        W("scalar", spe, outT_lastread[g % 2])
                        P.ps_pre_read("scalar", (0,))
                        oev = E("scalar", "activation", oslot[:], bank(0), AF.Identity,
                                bias=f32(0.0), scale=f32(1.0), inc=sse)
                        P.ps_post_read((0,), sse, oev)
                        W("tensor", sse, oev)
                        P.ps_pre_write("tensor", (1,))
                        ro = E("tensor", "matmul", bank(1)[0:64, :],
                               basisT[:, 3 * 1024 + 64 * h:3 * 1024 + 64 * (h + 1)],
                               oslot[:], start=True, stop=True, inc=spe)
                        outT_lastread[g % 2] = ro
                        P.ps_post_write((1,), spe, ro)
                        W("vector", sve, resO_last)
                        P.ps_pre_read("vector", (1,))
                        ra = E("vector", "tensor_add",
                               resOa[0:64, 512 * g:512 * (g + 1)],
                               resOa[0:64, 512 * g:512 * (g + 1)],
                               bank(1)[0:64, :], inc=sve)
                        resO_last = ra
                        P.ps_post_read((1,), sve, ra)

                pending = None
                for qi in range(16):
                    state = head_part(qi)
                    if pending is not None:
                        tail_part(*pending)
                    v3 = finish_part(state)
                    pending = (qi, v3)
                tail_part(*pending)

            resOb = probs[0]
            W("vector", sve, resO_last)
            W("vector", spe, spe.n)
            rb = E("vector", "tensor_copy", resOb[64:128, 0:2048], resOa[0:64, :], inc=sve)
            for t in range(NT):
                for c in range(4):
                    pb = 2 + ((4 * t + c) % 2)
                    W("tensor", sve, rb)
                    P.ps_pre_write("tensor", (pb,))
                    mm = E("tensor", "matmul", bank(pb),
                           resOb[64:128, 128 * t:128 * (t + 1)],
                           wTvo[64:128, 512 * c:512 * (c + 1)],
                           start=True, stop=True, inc=spe)
                    P.ps_post_write((pb,), spe, mm)
                    W("vector", spe, mm)
                    if x_add_done[t]:
                        W("vector", sve, x_add_done[t])
                    P.ps_pre_read("vector", (pb,))
                    xs = X[:, 2048 * t + 512 * c:2048 * t + 512 * (c + 1)]
                    va = E("vector", "tensor_add", xs, xs, bank(pb), inc=sve)
                    x_add_done[t] = va
                    P.ps_post_read((pb,), sve, va)
            A.release()
            A.release()

        sgw = [Sem(ctx.enter_context(nc.semaphore(f"sgw{i}")), "swdge") for i in range(2)]
        sgu = [Sem(ctx.enter_context(nc.semaphore(f"sgu{i}")), "swdge") for i in range(2)]
        sgd = Sem(ctx.enter_context(nc.semaphore("sgd")), "swdge")
        sgr = Sem(ctx.enter_context(nc.semaphore("sgr")), "swdge")
        sth = [Sem(ctx.enter_context(nc.semaphore(f"sth{i}")), "hwdge") for i in range(2)]
        shr = [Sem(ctx.enter_context(nc.semaphore(f"shr{i}")), "hwdge") for i in range(2)]
        all_sems.extend([sg, sl, st, spe, sve, sse, spo, sgr, sgd] + sgw + sgu + sth + shr)

        def unit_list():
            us = [("s", NE * DE, NJ_S, inp["Ws_gate"], inp["Ws_up"], inp["Ws_down"])]
            for e in range(NE):
                us.append((e, e * DE, NJ_E, inp["We_gate"][e], inp["We_up"][e],
                           inp["We_down"][e]))
            return us

        ffn_state = {"wr_loaded": 0, "comb_lastread": 0, "d_read_done": (0, 0)}

        def emit_ffn(call):
            barrier()
            A.mark()
            RT = A.f(128)
            RTP = A.f(128)
            wrb = A.b(128)
            g0 = DMA("gpsimd", wrb.rearrange("p (d r) -> p d r", d=ND),
                     inp["W_router"].rearrange("(d p) r -> p d r", p=128), sgr)

            units = unit_list()
            gH_done = []
            h_written = []

            for half in range(2):
                if half == 1:
                    barrier()
                tiles = list(range(8 * half, 8 * half + 8))
                A.mark()
                lnT_h = A.b(ND * 1024)

                def x_ready(t):
                    W("vector", sve, x_add_done[t])
                    W("scalar", sve, x_add_done[t])

                ln_done = emit_ln(x_ready, tiles=tiles, dst_lnT=lnT_h)

                W("tensor", sgr, sgr.n)
                rt_ev = None
                A.mark()
                rstage = A.f(1024)
                for c2 in range(2):
                    P.ps_pre_write("tensor", (0,))
                    W("tensor", sve, ln_done)
                    for d in range(ND):
                        E("tensor", "matmul", bank(0)[0:8, :],
                          wrb[:, 8 * d:8 * (d + 1)],
                          lnT_h[:, 1024 * d + 512 * c2:1024 * d + 512 * (c2 + 1)],
                          start=(d == 0), stop=(d == ND - 1),
                          inc=(spe if d == ND - 1 else None))
                    P.ps_post_write((0,), spe, spe.n)
                    P.ps_pre_read("vector", (0,))
                    rev = E("vector", "tensor_copy", rstage[0:8, 512 * c2:512 * (c2 + 1)],
                            bank(0)[0:8, :], inc=sve)
                    P.ps_post_read((0,), sve, rev)
                W("tensor", sve, sve.n)
                W("tensor", sl, p0_loads)
                P.ps_pre_write("tensor", (1,))
                for i8 in range(8):
                    E("tensor", "transpose", bank(1)[:, 8 * i8:8 * (i8 + 1)],
                      rstage[0:8, 128 * i8:128 * (i8 + 1)], IDF[0:8, 0:8],
                      inc=(spe if i8 == 7 else None))
                P.ps_post_write((1,), spe, spe.n)
                P.ps_pre_read("vector", (1,))
                rt_ev = E("vector", "tensor_copy", RT[:, 64 * half:64 * (half + 1)],
                          bank(1)[:, 0:64], inc=sve)
                P.ps_post_read((1,), sve, rt_ev)
                A.release()

                cNM = SMALL[:, 28:29]
                cRS = SMALL[:, 29:30]
                cM1 = SMALL[:, 30:31]
                cM2 = SMALL[:, 31:32]
                tEQ = SMALL[:, 32:40]
                tMK = SMALL[:, 40:48]
                for i, t in enumerate(tiles):
                    rts = RT[:, 8 * (t % 8) + 64 * half:8 * (t % 8) + 64 * half + 8]
                    rtp = RTP[:, 8 * (t % 8) + 64 * half:8 * (t % 8) + 64 * half + 8]
                    W("vector", sve, rt_ev)
                    if i > 0:
                        W("vector", sve, comb_w)
                        W("vector", sse, exp_done)
                    v1 = E("vector", "reduce_max", cNM, rts, axis=AX.X,
                           negate=True, inc=sve)
                    W("scalar", sve, v1)
                    exp_done = E("scalar", "activation", rtp, rts, AF.Exp,
                                 bias=cNM, scale=f32(1.0), accum_out=cRS, inc=sse)
                    W("vector", sse, exp_done)
                    v2 = E("vector", "reciprocal", cRS, cRS, inc=sve)
                    W("vector", sve, v2)
                    v3 = E("vector", "tensor_scalar", rtp, rtp, cRS, None,
                           op0=ALU.mult, inc=sve)
                    W("vector", sve, v3)
                    v4 = E("vector", "reduce_max", cM1, rtp, axis=AX.X, inc=sve)
                    W("vector", sve, v4)
                    v5 = E("vector", "tensor_scalar", tEQ, rtp, cM1, None,
                           op0=ALU.is_equal, inc=sve)
                    W("vector", sve, v5)
                    v6 = E("vector", "scalar_tensor_tensor", tMK, tEQ, -4.0, rtp,
                           op0=ALU.mult, op1=ALU.add, inc=sve)
                    W("vector", sve, v6)
                    v7 = E("vector", "reduce_max", cM2, tMK, axis=AX.X, inc=sve)
                    W("vector", sve, v7)
                    v8 = E("vector", "tensor_scalar", tEQ, rtp, cM2, None,
                           op0=ALU.is_ge, inc=sve)
                    W("vector", sve, v8)
                    W("vector", sve, ffn_state["comb_lastread"])
                    comb_w = E("vector", "tensor_mul", COMB[:, 8 * t:8 * (t + 1)],
                               tEQ, rtp, inc=sve)
                comb_done = sve.n

                wgb = [A.b(2048), A.b(2048)]
                wub = [A.b(2048), A.b(2048)]
                sgl = [A.f(1024), A.f(1024)]
                hjb = [A.b(1024), A.b(1024)]
                mm_done = {}
                silu_done = {}
                hmul_done = {}
                hdma_done = {}
                jj = 0
                W("gpsimd", sse, sse.n)
                W("gpsimd", sve, sve.n)
                W("gpsimd", spe, spe.n)
                if call > 0:
                    W("sync", shr[0], ffn_state["d_read_done"][0])
                    W("sync", shr[1], ffn_state["d_read_done"][1])
                for (uname, row0, nj, gsrc, usrc, dsrc) in units:
                    for j in range(nj):
                        sp = jj % 2
                        W("gpsimd", spe, mm_done.get(("g", sp), 0))
                        for dh in range(2):
                            DMA("gpsimd",
                                wgb[sp].rearrange("p (d j) -> p d j", d=ND)[:, 8 * dh:8 * (dh + 1), :],
                                gsrc[1024 * dh:1024 * (dh + 1), 128 * j:128 * (j + 1)].rearrange("(d p) j -> p d j", p=128),
                                sgw[sp])
                        W("gpsimd", spe, mm_done.get(("u", sp), 0))
                        for dh in range(2):
                            DMA("gpsimd",
                                wub[sp].rearrange("p (d j) -> p d j", d=ND)[:, 8 * dh:8 * (dh + 1), :],
                                usrc[1024 * dh:1024 * (dh + 1), 128 * j:128 * (j + 1)].rearrange("(d p) j -> p d j", p=128),
                                sgu[sp])
                        gb = (0, 1) if sp == 0 else (4, 5)
                        ub = (2, 3) if sp == 0 else (6, 7)
                        W("tensor", sgw[sp], sgw[sp].n)
                        W("tensor", sve, ln_done)
                        P.ps_pre_write("tensor", gb)
                        for c2 in range(2):
                            for d in range(ND):
                                E("tensor", "matmul", bank(gb[c2]),
                                  wgb[sp][:, 128 * d:128 * (d + 1)],
                                  lnT_h[:, 1024 * d + 512 * c2:1024 * d + 512 * (c2 + 1)],
                                  start=(d == 0), stop=(d == ND - 1),
                                  inc=(spe if (c2 == 1 and d == ND - 1) else None))
                        mm_done[("g", sp)] = spe.n
                        P.ps_post_write(gb, spe, spe.n)
                        W("tensor", sgu[sp], sgu[sp].n)
                        P.ps_pre_write("tensor", ub)
                        for c2 in range(2):
                            for d in range(ND):
                                E("tensor", "matmul", bank(ub[c2]),
                                  wub[sp][:, 128 * d:128 * (d + 1)],
                                  lnT_h[:, 1024 * d + 512 * c2:1024 * d + 512 * (c2 + 1)],
                                  start=(d == 0), stop=(d == ND - 1),
                                  inc=(spe if (c2 == 1 and d == ND - 1) else None))
                        mm_done[("u", sp)] = spe.n
                        P.ps_post_write(ub, spe, spe.n)
                        W("scalar", sve, hmul_done.get(sp, 0))
                        P.ps_pre_read("scalar", gb)
                        sd = E("scalar", "activation", sgl[sp], bank(gb[0], 2),
                               AF.Silu, bias=f32(0.0), scale=f32(1.0), inc=sse)
                        silu_done[sp] = sd
                        P.ps_post_read(gb, sse, sd)
                        W("vector", sse, sd)
                        W("vector", sth[sp], hdma_done.get(sp, 0))
                        P.ps_pre_read("vector", ub)
                        hm = E("vector", "tensor_mul", hjb[sp], sgl[sp],
                               bank(ub[0], 2), inc=sve)
                        hmul_done[sp] = hm
                        P.ps_post_read(ub, sve, hm)
                        W("sync", sve, hm)
                        hdma_done[sp] = DMA("sync",
                                            h_scr[row0 + 128 * j:row0 + 128 * (j + 1),
                                                  1024 * half:1024 * (half + 1)],
                                            hjb[sp], sth[sp])
                        jj += 1
                gH_done.append(spe.n)
                h_written.append((sth[0].n, sth[1].n))
                A.release()

            def emit_sub(uname, row0, nj, dsrc, oh, ow, wd_ld, sp):
                shared = uname == "s"
                nb = ow // 512
                for ti, t in enumerate(range(NT)):
                    hsp = ti % 2
                    W("sync", spe, hb_used.get(hsp, 0))
                    rb = DMA("sync",
                             hb_e[hsp].rearrange("p (j c) -> p j c", j=NJ_S)[:, 0:nj, :],
                             h_scr[row0:row0 + nj * 128, 128 * t:128 * (t + 1)].rearrange("(j p) c -> p j c", p=128),
                             shr[hsp])
                    ybs = ((0, 1) if ti % 2 == 0 else (4, 5))[:nb]
                    W("tensor", shr[hsp], rb)
                    W("tensor", sgdp[sp], wd_ld)
                    P.ps_pre_write("tensor", ybs)
                    for j in range(nj):
                        for cc in range(nb):
                            E("tensor", "matmul", bank(ybs[cc]),
                              hb_e[hsp][:, 128 * j:128 * (j + 1)],
                              wd_sb[sp][:, ow * j + 512 * cc:ow * j + 512 * (cc + 1)],
                              start=(j == 0), stop=(j == nj - 1),
                              skip_group_check=True,
                              inc=(spe if (j == nj - 1 and cc == nb - 1) else None))
                    hb_used[hsp] = spe.n
                    P.ps_post_write(ybs, spe, spe.n)
                    W("vector", spe, spe.n)
                    W("vector", sve, x_add_done[t])
                    P.ps_pre_read("vector", ybs)
                    xsl = X[:, 2048 * t + ow * oh:2048 * t + ow * oh + ow]
                    if shared:
                        va = E("vector", "tensor_add", xsl, xsl, bank(ybs[0], nb),
                               inc=sve)
                    else:
                        va = E("vector", "scalar_tensor_tensor", xsl,
                               bank(ybs[0], nb), COMB[:, 8 * t + uname:8 * t + uname + 1],
                               xsl, op0=ALU.mult, op1=ALU.add, inc=sve)
                    x_add_done[t] = va
                    P.ps_post_read(ybs, sve, va)
                wd_used[sp] = spe.n

            barrier()
            A.mark()
            wd_sb = [A.b(NJ_E * 1024), A.b(NJ_E * 1024)]
            hb_e = [A.b(NJ_S * 128), A.b(NJ_S * 128)]
            W("sync", sth[0], sth[0].n)
            W("sync", sth[1], sth[1].n)
            wd_used = {}
            hb_used = {}
            sgdp = [sgw[0], sgw[1]]
            subunits = []
            for (uname, row0, nj, gsrc, usrc, dsrc) in units:
                ow = 512 if uname == "s" else 1024
                for oh in range(2048 // ow):
                    subunits.append((uname, row0, nj, dsrc, oh, ow))
            for si, su in enumerate(subunits):
                uname, row0, nj, dsrc, oh, ow = su
                sp = si % 2
                W("gpsimd", spe, wd_used.get(sp, 0))
                for j in range(nj):
                    DMA("gpsimd", wd_sb[sp][:, ow * j:ow * (j + 1)],
                        dsrc[128 * j:128 * (j + 1), ow * oh:ow * (oh + 1)],
                        sgdp[sp])
                subunits[si] = (uname, row0, nj, dsrc, oh, ow, sgdp[sp].n, sp)
                if si >= 1:
                    emit_sub(*subunits[si - 1])
            emit_sub(*subunits[-1])
            ffn_state["comb_lastread"] = sve.n
            ffn_state["d_read_done"] = (shr[0].n, shr[1].n)
            A.release()
            A.release()

        swq = [Sem(ctx.enter_context(nc.semaphore(f"swq{i}")), "hwdge") for i in range(2)]
        swk = Sem(ctx.enter_context(nc.semaphore("swk")), "hwdge")
        stw = [Sem(ctx.enter_context(nc.semaphore(f"stw{i}")), "hwdge") for i in range(2)]
        sgt = [Sem(ctx.enter_context(nc.semaphore(f"sgt{i}")), "swdge") for i in range(2)]
        sot = [Sem(ctx.enter_context(nc.semaphore(f"sot{i}")), "hwdge") for i in range(2)]
        all_sems.extend(swq + [swk] + stw + sgt + sot)

        def emit_wtprep():
            barrier()
            A.mark()
            wtmp = [A.b(2048), A.b(2048)]
            wstage = [A.b(2048), A.b(2048)]
            blocks = []
            for ob in range(16):
                blocks.append((inp["Wq"], ob, 128 * ob))
            for ob in range(4):
                blocks.append((inp["Wk"], ob, 2048 + 128 * ob))
            for ob in range(4):
                blocks.append((inp["Wv"], ob, 2560 + 128 * ob))
            for ob in range(16):
                blocks.append((inp["Wo"], ob, 3072 + 128 * ob))
            tr_done = {}
            ev_done = {}
            dma_out = {}
            for i, (src, ob, colb) in enumerate(blocks):
                sp = i % 2
                if i >= 2:
                    W("gpsimd", spe, tr_done[i - 2])
                DMA("gpsimd", wtmp[sp], src[128 * ob:128 * (ob + 1), :], sgt[sp])
                W("tensor", sgt[sp], sgt[sp].n)
                pb = (0, 1) if sp == 0 else (2, 3)
                P.ps_pre_write("tensor", pb)
                psb = bankb(pb[0], 2)
                for j in range(ND):
                    kw = dict(inc=spe) if j == ND - 1 else {}
                    E("tensor", "transpose", psb[:, 128 * j:128 * (j + 1)],
                      wtmp[sp][:, 128 * j:128 * (j + 1)], IDB[:], **kw)
                tr_done[i] = spe.n
                P.ps_post_write(pb, spe, spe.n)
                if i >= 2:
                    W("vector", stw[sp], dma_out[i - 2])
                P.ps_pre_read("vector", pb)
                ev = E("vector", "tensor_copy", wstage[sp][:], psb, inc=sve)
                ev_done[i] = ev
                P.ps_post_read(pb, sve, ev)
                W("sync", sve, ev)
                dma_out[i] = DMA("sync",
                                 wT_scr[:, colb:colb + 128].rearrange("(j p) c -> p j c", p=128),
                                 wstage[sp].rearrange("p (j c) -> p j c", j=ND), stw[sp])
            bq_tm = SMALL[:, 80:96]
            bk_tm = SMALL[:, 96:100]
            bv_tm = SMALL[:, 100:104]
            bq_sw = SMALL[:, 12:28]
            bk_sw = SMALL[:, 116:120]
            for hh in range(16):
                DMA("sync", bq_tm[:, hh:hh + 1], inp["bq"][128 * hh:128 * (hh + 1)].unsqueeze(1), sl)
                DMA("sync", bq_sw[0:64, hh:hh + 1], inp["bq"][128 * hh + 64:128 * (hh + 1)].unsqueeze(1), sl)
                DMA("sync", bq_sw[64:128, hh:hh + 1], inp["bq"][128 * hh:128 * hh + 64].unsqueeze(1), sl)
            for gg in range(4):
                DMA("sync", bk_tm[:, gg:gg + 1], inp["bk"][128 * gg:128 * (gg + 1)].unsqueeze(1), sl)
                DMA("sync", bk_sw[0:64, gg:gg + 1], inp["bk"][128 * gg + 64:128 * (gg + 1)].unsqueeze(1), sl)
                DMA("sync", bk_sw[64:128, gg:gg + 1], inp["bk"][128 * gg:128 * gg + 64].unsqueeze(1), sl)
                DMA("sync", bv_tm[:, gg:gg + 1], inp["bv"][128 * gg:128 * (gg + 1)].unsqueeze(1), sl)
            A.release()
            return sl.n

        def emit_attn_b():
            bias_loaded = emit_wtprep()
            negmr_sv = SMALL[:, 48:64]
            rstd_sv = SMALL[:, 64:80]
            bq_tm = SMALL[:, 80:96]
            bk_tm = SMALL[:, 96:100]
            bv_tm = SMALL[:, 100:104]
            bq_sw = SMALL[:, 12:28]
            bk_sw = SMALL[:, 116:120]
            cMx = SMALL[:, 104:108]
            cMxA = SMALL[:, 108:112]
            cMxB = SMALL[:, 112:116]

            for pss in range(4):
                barrier()
                A.mark()
                qT = A.b(4 * 2048)
                kT = A.b(2048)
                vg = A.b(2048)
                A.mark()
                lnchunk = A.b(ND * 512)
                wqb = [A.b(2048), A.b(2048)]
                wkv = A.b(4096)
                tblc = A.f(512)
                tbls = A.f(512)
                tmp1 = A.f(512)
                tmp2 = A.f(512)
                g = pss

                ln_ev = {}
                rope_done = 0
                tbl_used = 0
                wkv_used = 0
                wq_used = {}
                v_ev = 0
                for c in range(4):
                    tiles = list(range(4 * c, 4 * c + 4))

                    def x_ready(t):
                        W("vector", sve, x_add_done[t])
                        W("scalar", sve, x_add_done[t])

                    lg = getattr(emit_ln, "last_tr", 0) if c > 0 else 0
                    if pss == 0:
                        lnd = emit_ln(x_ready, tiles=tiles, dst_lnT=lnchunk,
                                      stats_save=(negmr_sv, rstd_sv), lnbf_guard=lg)
                    else:
                        lnd = emit_ln(x_ready, tiles=tiles, dst_lnT=lnchunk,
                                      stats_use=(negmr_sv, rstd_sv), lnbf_guard=lg)
                    W("sync", sve, rope_done)
                    DMA("sync", tblc, inp["cosq"][:, 512 * c:512 * (c + 1)], sl)
                    DMA("sync", tbls, inp["sinq"][:, 512 * c:512 * (c + 1)], sl)
                    W("sync", spe, wkv_used)
                    DMA("sync", wkv[:, 0:2048].rearrange("p (d j) -> p d j", d=ND)[:, 0:8, :],
                        wT_scr[0:1024, 2048 + 128 * g:2048 + 128 * (g + 1)].rearrange("(d p) c -> p d c", p=128), swk)
                    DMA("sync", wkv[:, 0:2048].rearrange("p (d j) -> p d j", d=ND)[:, 8:16, :],
                        wT_scr[1024:2048, 2048 + 128 * g:2048 + 128 * (g + 1)].rearrange("(d p) c -> p d c", p=128), swk)
                    DMA("sync", wkv[:, 2048:4096].rearrange("p (d j) -> p d j", d=ND)[:, 0:8, :],
                        wT_scr[0:1024, 2560 + 128 * g:2560 + 128 * (g + 1)].rearrange("(d p) c -> p d c", p=128), swk)
                    kvld = DMA("sync", wkv[:, 2048:4096].rearrange("p (d j) -> p d j", d=ND)[:, 8:16, :],
                               wT_scr[1024:2048, 2560 + 128 * g:2560 + 128 * (g + 1)].rearrange("(d p) c -> p d c", p=128), swk)

                    def rope_evac(pbk, bias_col, bias_sw, dst):
                        nonlocal rope_done
                        W("vector", sl, sl.n)
                        W("vector", sl, bias_loaded)
                        W("vector", sve, rope_done)
                        P.ps_pre_read("vector", (pbk,))
                        r1 = E("vector", "scalar_tensor_tensor", tmp1, bank(pbk),
                               bias_col, tblc, op0=ALU.add, op1=ALU.mult, inc=sve)
                        r2 = E("vector", "scalar_tensor_tensor", tmp2[0:64, :],
                               bank(pbk)[64:128, :], bias_sw[0:64, :], tbls[0:64, :],
                               op0=ALU.add, op1=ALU.mult, inc=sve)
                        r3 = E("vector", "scalar_tensor_tensor", tmp2[64:128, :],
                               bank(pbk)[0:64, :], bias_sw[64:128, :], tbls[64:128, :],
                               op0=ALU.add, op1=ALU.mult, inc=sve)
                        W("vector", sve, r3)
                        r4 = E("vector", "tensor_add", dst, tmp1, tmp2, inc=sve)
                        rope_done = r4
                        P.ps_post_read((pbk,), sve, r4)
                        return r4

                    W("tensor", swk, kvld)
                    W("tensor", sve, lnd)
                    pb = 4
                    P.ps_pre_write("tensor", (pb,))
                    for d in range(ND):
                        E("tensor", "matmul", bank(pb),
                          wkv[:, 128 * d:128 * (d + 1)],
                          lnchunk[:, 512 * d:512 * (d + 1)],
                          start=(d == 0), stop=(d == ND - 1),
                          inc=(spe if d == ND - 1 else None))
                    P.ps_post_write((pb,), spe, spe.n)
                    rope_evac(pb, bk_tm[:, g:g + 1], bk_sw[:, g:g + 1], kT[:, 512 * c:512 * (c + 1)])
                    for hl in range(4):
                        h = 4 * pss + hl
                        sp = hl % 2
                        W("sync", spe, wq_used.get(sp, 0))
                        DMA("sync", wqb[sp].rearrange("p (d j) -> p d j", d=ND)[:, 0:8, :],
                            wT_scr[0:1024, 128 * h:128 * (h + 1)].rearrange("(d p) c -> p d c", p=128), swq[sp])
                        qld = DMA("sync", wqb[sp].rearrange("p (d j) -> p d j", d=ND)[:, 8:16, :],
                                  wT_scr[1024:2048, 128 * h:128 * (h + 1)].rearrange("(d p) c -> p d c", p=128), swq[sp])
                        pbq = 5 + sp
                        W("tensor", swq[sp], qld)
                        W("tensor", sve, lnd)
                        P.ps_pre_write("tensor", (pbq,))
                        for d in range(ND):
                            E("tensor", "matmul", bank(pbq),
                              wqb[sp][:, 128 * d:128 * (d + 1)],
                              lnchunk[:, 512 * d:512 * (d + 1)],
                              start=(d == 0), stop=(d == ND - 1),
                              inc=(spe if d == ND - 1 else None))
                        wq_used[sp] = spe.n
                        P.ps_post_write((pbq,), spe, spe.n)
                        rope_evac(pbq, bq_tm[:, h:h + 1], bq_sw[:, h:h + 1],
                                  qT[:, 2048 * hl + 512 * c:2048 * hl + 512 * (c + 1)])
                    W("tensor", swk, kvld)
                    P.ps_pre_write("tensor", (7,))
                    for tt in range(4):
                        for d in range(ND):
                            E("tensor", "matmul", bank(7)[:, 128 * tt:128 * (tt + 1)],
                              lnchunk[:, 512 * d + 128 * tt:512 * d + 128 * (tt + 1)],
                              wkv[:, 2048 + 128 * d:2048 + 128 * (d + 1)],
                              start=(d == 0), stop=(d == ND - 1),
                              inc=(spe if (tt == 3 and d == ND - 1) else None))
                    wkv_used = spe.n
                    P.ps_post_write((7,), spe, spe.n)
                    P.ps_pre_read("vector", (7,))
                    v_ev = E("vector", "tensor_copy", vg[:, 512 * c:512 * (c + 1)],
                             bank(7), inc=sve)
                    P.ps_post_read((7,), sve, v_ev)
                qkv_done = sve.n
                A.release()
                barrier()

                A.mark()
                probs = [A.b(2048), A.b(2048)]
                probsT = A.b(ND * 512)
                outTh = [A.b(512), A.b(512)]
                probs_T_done = {}
                mx_lastread = {}
                bias_lastread = {}
                rq_lastread = {}
                outT_dma = {}
                pz = 0
                def b_head_part(hl, qi):
                    qTh_ = qT[:, 2048 * hl:2048 * (hl + 1)]
                    par = qi % 2
                    pslot = probs[par]
                    gqi = 64 * pss + 16 * hl + qi
                    nvalid = qi + 1
                    wtot = 128 * nvalid
                    nch = qi // 4 + 1
                    nhalf = (nch + 1) // 2
                    slot_free = probs_T_done.get(gqi - 2, 0)
                    cb = 104 if par == 0 else 120
                    mx = SMALL[:, cb:cb + 2]
                    cbias = SMALL[:, cb + 2:cb + 4]
                    csum = SMALL[:, cb + 4:cb + 6]
                    cfac = SMALL[:, cb + 6:cb + 8]
                    mx_free = mx_lastread.get(par, 0)
                    bias_free = bias_lastread.get(par, 0)
                    last_exp = 0
                    for hf in range(nhalf):
                        w_half = min(wtot - 1024 * hf, 1024)
                        pb0 = 4 + 2 * hf
                        pbs = (pb0,) if w_half <= 512 else (pb0, pb0 + 1)
                        for c2 in range(2):
                            c = 2 * hf + c2
                            if c >= nch:
                                continue
                            wd = min(wtot - 512 * c, 512)
                            if c == 0:
                                W("tensor", sve, qkv_done)
                            P.ps_pre_write("tensor", (pb0 + c2,))
                            mm = E("tensor", "matmul", bank(pb0 + c2)[:, 0:wd],
                                   qTh_[:, 128 * qi:128 * (qi + 1)],
                                   kT[:, 512 * c:512 * c + wd],
                                   start=True, stop=True, inc=spe)
                            P.ps_post_write((pb0 + c2,), spe, mm)
                            if c == nch - 1:
                                W("vector", spe, mm)
                                W("vector", sl, p0_loads)
                                P.ps_pre_read("vector", (pb0 + c2,))
                                dm = E("vector", "tensor_add",
                                       bank(pb0 + c2)[:, wd - 128:wd],
                                       bank(pb0 + c2)[:, wd - 128:wd],
                                       MASKN[:], inc=sve)
                                P.ps_post_read((pb0 + c2,), sve, dm)
                        if hf == 0:
                            W("vector", sve, mx_free)
                        W("vector", sve, sve.n)
                        P.ps_pre_read("vector", pbs)
                        vm = E("vector", "reduce_max", mx[:, hf:hf + 1],
                               bank(pb0, len(pbs))[:, 0:w_half], axis=AX.X, inc=sve)
                        P.ps_post_read(pbs, sve, vm)
                        W("vector", sve, vm)
                        if hf == 0:
                            W("vector", sse, bias_free)
                        vb = E("vector", "tensor_scalar", cbias[:, hf:hf + 1],
                               mx[:, hf:hf + 1], float(-SCALE), None,
                               op0=ALU.mult, inc=sve)
                        W("scalar", sve, vb)
                        if hf == 0:
                            W("scalar", spe, slot_free)
                        le = E("scalar", "activation",
                               pslot[:, 1024 * hf:1024 * hf + w_half],
                               bank(pb0, len(pbs))[:, 0:w_half], AF.Exp,
                               bias=cbias[:, hf:hf + 1], scale=SCALE,
                               accum_out=csum[:, hf:hf + 1], inc=sse)
                        last_exp = le
                        P.ps_post_read(pbs, sse, le)
                    bias_lastread[par] = last_exp
                    return (hl, qi, par, pslot, gqi, mx, cbias, csum, cfac,
                            nhalf, nvalid, last_exp)

                def b_finish_part(stt_):
                    (hl, qi, par, pslot, gqi, mx, cbias, csum, cfac,
                     nhalf, nvalid, last_exp) = stt_
                    wtot = 128 * nvalid
                    W("vector", sse, last_exp)
                    if nhalf == 1:
                        v5 = E("vector", "tensor_copy", csum[:, 0:1],
                               csum[:, 0:1], inc=sve)
                        mx_lastread[par] = v5
                        W("vector", sve, v5)
                        v6 = E("vector", "reciprocal", csum[:, 0:1], csum[:, 0:1],
                               inc=sve)
                        W("vector", sve, v6)
                        v8 = E("vector", "tensor_scalar",
                               pslot[:, 0:wtot], pslot[:, 0:wtot],
                               csum[:, 0:1], None, op0=ALU.mult, inc=sve)
                        return v8
                    v1 = E("vector", "tensor_max", cbias[:, 0:1], mx[:, 0:1],
                           mx[:, 1:2], inc=sve)
                    W("vector", sve, v1)
                    v2 = E("vector", "tensor_scalar", cbias[:, 0:1], cbias[:, 0:1],
                           -1.0, None, op0=ALU.mult, inc=sve)
                    W("vector", sve, v2)
                    v3 = E("vector", "tensor_scalar", cfac, mx,
                           cbias[:, 0:1], float(SCALE), op0=ALU.add,
                           op1=ALU.mult, inc=sve)
                    W("scalar", sve, v3)
                    s1 = E("scalar", "activation", cfac, cfac, AF.Exp,
                           bias=f32(0.0), scale=f32(1.0), inc=sse)
                    W("vector", sse, s1)
                    v4 = E("vector", "tensor_mul", csum, csum, cfac, inc=sve)
                    W("vector", sve, v4)
                    v5 = E("vector", "tensor_add", csum[:, 0:1], csum[:, 0:1],
                           csum[:, 1:2], inc=sve)
                    mx_lastread[par] = v5
                    W("vector", sve, v5)
                    v6 = E("vector", "reciprocal", csum[:, 0:1], csum[:, 0:1],
                           inc=sve)
                    W("vector", sve, v6)
                    v7 = E("vector", "tensor_scalar", cfac, cfac, csum[:, 0:1],
                           None, op0=ALU.mult, inc=sve)
                    W("vector", sve, v7)
                    v8 = E("vector", "tensor_scalar",
                           pslot[:, 0:1024], pslot[:, 0:1024],
                           cfac[:, 0:1], None, op0=ALU.mult, inc=sve)
                    W("vector", sve, v8)
                    v9 = E("vector", "tensor_scalar",
                           pslot[:, 1024:wtot], pslot[:, 1024:wtot],
                           cfac[:, 1:2], None, op0=ALU.mult, inc=sve)
                    return v9

                def b_tail_part(stt_, v3):
                    (hl, qi, par, pslot, gqi, mx, cbias, csum, cfac,
                     nhalf, nvalid, last_exp) = stt_
                    h = 4 * pss + hl
                    qq = qi % 4
                    grp = qi // 4
                    W("tensor", sve, v3)
                    P.ps_pre_write("tensor", (2, 3))
                    ptb = bankb(2, 2)
                    nt_this = min(nvalid, ND)
                    for j in range(nt_this):
                        kw = dict(inc=spe) if j == nt_this - 1 else {}
                        E("tensor", "transpose", ptb[:, 128 * j:128 * (j + 1)],
                          pslot[:, 128 * j:128 * (j + 1)], IDB[:], **kw)
                    probs_T_done[gqi] = spe.n
                    P.ps_post_write((2, 3), spe, spe.n)
                    pT3 = probsT.rearrange("p (k q) -> p k q", k=ND)
                    ptb3 = ptb.rearrange("p (j c) -> p j c", j=ND)
                    nv_ = min(nt_this, 8)
                    pev2 = 0
                    P.ps_pre_read("vector", (2,))
                    pev = E("vector", "tensor_copy",
                            pT3[:, 0:nv_, 128 * qq:128 * (qq + 1)],
                            ptb3[:, 0:nv_, :], inc=sve)
                    P.ps_post_read((2,), sve, pev)
                    if nt_this > 8:
                        P.ps_pre_read("scalar", (3,))
                        pev2 = E("scalar", "activation",
                                 pT3[:, 8:nt_this, 128 * qq:128 * (qq + 1)],
                                 ptb3[:, 8:nt_this, :],
                                 AF.Identity, bias=f32(0.0), scale=f32(1.0), inc=sse)
                        P.ps_post_read((3,), sse, pev2)
                    for kb in range(4 * grp + qq + 1, 4 * grp + 4):
                        E("vector", "memset",
                          pT3[:, kb:kb + 1, 128 * qq:128 * (qq + 1)],
                          0.0, inc=sve)
                    if qq == 3:
                        W("tensor", sve, sve.n)
                        if pev2:
                            W("tensor", sse, pev2)
                        nkb = 4 * grp + 4
                        P.ps_pre_write("tensor", (0,))
                        for k in range(nkb):
                            E("tensor", "matmul", bank(0), vg[:, 128 * k:128 * (k + 1)],
                              probsT[:, 512 * k:512 * (k + 1)],
                              start=(k == 0), stop=(k == nkb - 1),
                              inc=(spe if k == nkb - 1 else None))
                        P.ps_post_write((0,), spe, spe.n)
                        oslot = outTh[grp % 2]
                        W("scalar", sot[grp % 2], outT_dma.get(grp % 2, 0))
                        W("scalar", sl, bias_loaded)
                        P.ps_pre_read("scalar", (0,))
                        oev = E("scalar", "activation", oslot[:], bank(0),
                                AF.Identity, bias=bv_tm[:, g:g + 1],
                                scale=f32(1.0), inc=sse)
                        P.ps_post_read((0,), sse, oev)
                        W("sync", sse, oev)
                        outT_dma[grp % 2] = DMA("sync",
                                                outT_scr[128 * h:128 * (h + 1),
                                                         512 * grp:512 * (grp + 1)],
                                                oslot[:], sot[grp % 2])

                pending = None
                for hl in range(4):
                    for qi in range(16):
                        stt_ = b_head_part(hl, qi)
                        if pending is not None:
                            b_tail_part(*pending)
                        v3 = b_finish_part(stt_)
                        pending = (stt_, v3)
                b_tail_part(*pending)
                A.release()
                A.release()

            barrier()
            A.mark()
            woT = A.b(ND * 2048)
            obuf = [A.b(2048), A.b(2048)]
            bo_row = A.b(2048)
            for d in range(ND):
                DMA("sync", woT[:, 2048 * d:2048 * (d + 1)],
                    wT_scr[128 * d:128 * (d + 1), 3072:5120], swk)
            DMA("gpsimd", bo_row[0:1, :], inp["bo"].unsqueeze(0), sg)
            wo_ld = swk.n
            bo_ld = sg.n
            ob_used = {}
            for t in range(NT):
                sp = t % 2
                W("sync", spe, ob_used.get(sp, 0))
                orb = DMA("sync", obuf[sp].rearrange("p (j c) -> p j c", j=ND),
                          outT_scr[:, 128 * t:128 * (t + 1)].rearrange("(j p) c -> p j c", p=128),
                          sot[sp])
                ybs = (0, 1, 2, 3) if sp == 0 else (4, 5, 6, 7)
                W("tensor", sot[sp], orb)
                W("tensor", swk, wo_ld)
                W("tensor", sg, bo_ld)
                P.ps_pre_write("tensor", ybs)
                for d in range(ND):
                    for cc in range(4):
                        E("tensor", "matmul", bank(ybs[cc]),
                          obuf[sp][:, 128 * d:128 * (d + 1)],
                          woT[:, 2048 * d + 512 * cc:2048 * d + 512 * (cc + 1)],
                          start=(d == 0), stop=False, skip_group_check=True)
                for cc in range(4):
                    E("tensor", "matmul", bank(ybs[cc]), ONEB[0:1, :],
                      bo_row[0:1, 512 * cc:512 * (cc + 1)],
                      start=False, stop=True, skip_group_check=True,
                      inc=(spe if cc == 3 else None))
                ob_used[sp] = spe.n
                P.ps_post_write(ybs, spe, spe.n)
                W("vector", spe, spe.n)
                W("vector", sve, x_add_done[t])
                P.ps_pre_read("vector", ybs)
                xsl = X[:, 2048 * t:2048 * (t + 1)]
                va = E("vector", "tensor_add", xsl, xsl, bank(ybs[0], 4), inc=sve)
                x_add_done[t] = va
                P.ps_post_read(ybs, sve, va)
            A.release()

        if STAGES >= 1:
            emit_attn_a()
        if STAGES >= 2:
            emit_ffn(0)
        if STAGES >= 3:
            emit_attn_b()
        if STAGES >= 4:
            emit_ffn(1)

        for t in range(NT):
            W("sync", sve, x_add_done[t] if x_add_done[t] else sve.n)
            DMA("sync", y_out[128 * t:128 * (t + 1), :], X[:, 2048 * t:2048 * (t + 1)], st)
        W("sync", st, st.n)
        W("sync", sl, sl.n)
        W("sync", sg, sg.n)

        block = ctx.enter_context(nc.Block())
        for eng in Prog.ENGINES:
            fns = P.q[eng]
            getattr(block, eng)(lambda e, fns=fns: [f(e) for f in fns])

    return nc


def make_constants():
    ident = np.eye(128, dtype=np.float32)
    import ml_dtypes
    identb = ident.astype(ml_dtypes.bfloat16)
    i0 = np.arange(64, dtype=np.float64)
    thetas = 10000.0 ** (-2.0 * i0 / 128.0)
    pos = np.arange(1, T + 1, dtype=np.float64)
    ang = pos[None, :] * thetas[:, None]
    cosq = np.concatenate([np.cos(ang), np.cos(ang)], axis=0).astype(np.float32)
    sinq = np.concatenate([-np.sin(ang), np.sin(ang)], axis=0).astype(np.float32)
    q = np.arange(128)[:, None]
    k = np.arange(128)[None, :]
    maskneg = np.where(k > q, -1e9, 0.0).astype(np.float32)
    return dict(identb=identb, identf=ident, cosq=cosq, sinq=sinq, maskneg=maskneg)


def make_in_maps(inputs):
    consts = make_constants()
    x = np.ascontiguousarray(inputs["L0_attn_a_hidden_states"], dtype=np.float32)
    wmap = dict(
        Wq=inputs["L0_attn_b_Wq"], bq=inputs["L0_attn_b_bq"],
        Wk=inputs["L0_attn_b_Wk"], bk=inputs["L0_attn_b_bk"],
        Wv=inputs["L0_attn_b_Wv"], bv=inputs["L0_attn_b_bv"],
        Wo=inputs["L0_attn_b_Wo"], bo=inputs["L0_attn_b_bo"],
        W_router=inputs["L0_ffn_W_router"],
        We_gate=inputs["L0_ffn_We_gate"], We_up=inputs["L0_ffn_We_up"],
        We_down=inputs["L0_ffn_We_down"],
        Ws_gate=inputs["L0_ffn_Ws_gate"], Ws_up=inputs["L0_ffn_Ws_up"],
        Ws_down=inputs["L0_ffn_Ws_down"],
    )
    for p in "qkvo":
        wmap[f"basis_{p}"] = inputs[f"L0_attn_a_basis_{p}"]
        wmap[f"phase_{p}"] = inputs[f"L0_attn_a_phase_{p}"]
        wmap[f"amp_{p}"] = inputs[f"L0_attn_a_amp_{p}"]
    wmap = {k: np.ascontiguousarray(v, dtype=np.float32) for k, v in wmap.items()}
    return [dict(x=np.ascontiguousarray(x[c]), **wmap, **consts)
            for c in range(x.shape[0])]


def kernel(**inputs):
    nc = build_nc()
    in_maps = make_in_maps(inputs)
    res = run_bass_kernel_spmd(nc, in_maps, core_ids=list(range(8)))
    return np.stack([np.asarray(r["y"], dtype=np.float32) for r in res.results], axis=0)


# version 12
# speedup vs baseline: 1.0423x; 1.0296x over previous
import math
import numpy as np
from contextlib import ExitStack

import concourse.bass as bass
import concourse.mybir as mybir
from concourse.bass_utils import run_bass_kernel_spmd

F32 = mybir.dt.float32
BF16 = mybir.dt.bfloat16
AF = mybir.ActivationFunctionType
ALU = mybir.AluOpType
AX = mybir.AxisListType

T = 2048
D = 2048
NT = 16
ND = 16
NHEAD = 16
DHEAD = 128
SCALE = np.float32(1.0 / math.sqrt(DHEAD))
EPS = 1e-5
DE = 1408
DS = 2816
NE = 8
NJ_E = DE // 128
NJ_S = DS // 128

STAGES = 4

f32 = np.float32


class Sem:
    def __init__(self, handle, kind):
        self.h = handle
        self.kind = kind
        self.n = 0


class Prog:
    ENGINES = ("tensor", "vector", "scalar", "gpsimd", "sync")

    def __init__(self):
        self.q = {e: [] for e in self.ENGINES}
        self.bw = [None] * 8
        self.br = [None] * 8

    def wait(self, eng, sem, val):
        if val <= 0:
            return
        assert val <= sem.n, f"wait target {val} > emitted count {sem.n}"
        self.q[eng].append(lambda e, s=sem.h, v=val: e.wait_ge(s, v))

    def op(self, eng, meth, *args, inc=None, incn=1, **kw):
        tgt = None
        if inc is not None:
            assert inc.kind == "eng"
            inc.n += incn
            tgt = inc.n

        def f(e, meth=meth, args=args, kw=kw, inc=inc, incn=incn):
            ins = getattr(e, meth)(*args, **kw)
            if inc is not None:
                ins.then_inc(inc.h, incn)

        self.q[eng].append(f)
        return tgt

    def dma(self, eng, dst, src, inc, **kw):
        assert inc.kind == ("swdge" if eng == "gpsimd" else "hwdge")
        inc.n += 16
        tgt = inc.n

        def f(e, dst=dst, src=src, kw=kw, inc=inc):
            e.dma_start(dst, src, **kw).then_inc(inc.h, 16)

        self.q[eng].append(f)
        return tgt

    def ps_pre_write(self, eng, banks):
        for b in banks:
            for rec in (self.br[b], self.bw[b]):
                if rec is not None:
                    self.wait(eng, rec[0], rec[1])

    def ps_post_write(self, banks, sem, cnt):
        for b in banks:
            self.bw[b] = (sem, cnt)
            self.br[b] = None

    def ps_pre_read(self, eng, banks):
        for b in banks:
            if self.bw[b] is not None:
                self.wait(eng, self.bw[b][0], self.bw[b][1])

    def ps_post_read(self, banks, sem, cnt):
        for b in banks:
            if self.br[b] is None or self.br[b][1] < cnt or self.br[b][0] is not sem:
                self.br[b] = (sem, cnt)


def build_nc():
    nc = bass.Bass("TRN2", target_bir_lowering=False)

    inp = {}

    def I(name, shape, dtype=F32):
        inp[name] = nc.dram_tensor(name, shape, dtype, kind="ExternalInput").ap()
        return inp[name]

    x_in = I("x", [T, D])
    for p in "qkvo":
        I(f"basis_{p}", [64, D])
        I(f"phase_{p}", [D, 64])
        I(f"amp_{p}", [D, 64])
    I("Wq", [D, D]); I("bq", [D])
    I("Wk", [512, D]); I("bk", [512])
    I("Wv", [512, D]); I("bv", [512])
    I("Wo", [D, D]); I("bo", [D])
    I("W_router", [D, NE])
    I("We_gate", [NE, D, DE]); I("We_up", [NE, D, DE]); I("We_down", [NE, DE, D])
    I("Ws_gate", [D, DS]); I("Ws_up", [D, DS]); I("Ws_down", [DS, D])
    I("identb", [128, 128], BF16)
    I("identf", [128, 128], F32)
    I("cosq", [128, T], F32)
    I("sinq", [128, T], F32)
    I("maskneg", [128, 128], F32)

    y_out = nc.dram_tensor("y", [T, D], F32, kind="ExternalOutput").ap()

    h_scr = nc.dram_tensor("h_scr", [NE * DE + DS, T], BF16, kind="Internal").ap()
    outT_scr = nc.dram_tensor("outT_scr", [D, T], BF16, kind="Internal").ap()
    wT_scr = nc.dram_tensor("wT_scr", [D, 2 * D + 1024], BF16, kind="Internal").ap()

    P = Prog()

    with ExitStack() as ctx:
        def sb(name, shape, dtype=F32):
            return ctx.enter_context(nc.sbuf_tensor(name, shape, dtype))

        X = sb("X", [128, NT * 2048])
        IDB = sb("IDB", [128, 128], BF16)
        IDF = sb("IDF", [128, 128], F32)
        ONEB = sb("ONEB", [128, 128], BF16)
        ONEF = sb("ONEF", [128, 2], F32)
        MASKN = sb("MASKN", [128, 128], F32)
        SMALL = sb("SMALL", [128, 128], F32)
        COMB = sb("COMB", [128, 128], F32)
        ARENA = sb("ARENA", [128, 19776], F32)

        PS = ctx.enter_context(nc.psum_tensor("PS", [128, 4096], F32))

        def bank(b, n=1):
            return PS[:, 512 * b: 512 * (b + n)]

        def bankb(b, n=1):
            return PS[:, 512 * b: 512 * (b + n)].bitcast(BF16)

        sg = Sem(ctx.enter_context(nc.semaphore("sg")), "swdge")
        sl = Sem(ctx.enter_context(nc.semaphore("sl")), "hwdge")
        st = Sem(ctx.enter_context(nc.semaphore("st")), "hwdge")
        spe = Sem(ctx.enter_context(nc.semaphore("spe")), "eng")
        sve = Sem(ctx.enter_context(nc.semaphore("sve")), "eng")
        sse = Sem(ctx.enter_context(nc.semaphore("sse")), "eng")
        spo = Sem(ctx.enter_context(nc.semaphore("spo")), "eng")

        class Arena:
            def __init__(self, t, n):
                self.t, self.N, self.off, self.marks = t, n, 0, []

            def f(self, n):
                ap = self.t[:, self.off:self.off + n]
                self.off += n
                assert self.off <= self.N, f"arena overflow {self.off}"
                return ap

            def b(self, n):
                return self.f(n // 2).bitcast(BF16)

            def mark(self):
                self.marks.append(self.off)

            def release(self):
                self.off = self.marks.pop()

        A = Arena(ARENA, 19776)

        E, W, DMA = P.op, P.wait, P.dma

        all_sems = []

        def barrier():
            for eng in Prog.ENGINES:
                for s in all_sems:
                    W(eng, s, s.n)
            P.bw = [None] * 8
            P.br = [None] * 8

        for t in range(NT):
            DMA("sync", X[:, 2048 * t:2048 * (t + 1)], x_in[128 * t:128 * (t + 1), :], sl)
        DMA("sync", IDB[:], inp["identb"], sl)
        DMA("sync", IDF[:], inp["identf"], sl)
        DMA("sync", MASKN[:], inp["maskneg"], sl)
        p0_loads = sl.n
        E("vector", "memset", ONEB[:], 1.0, inc=sve)
        ones_ready = E("vector", "memset", ONEF[:], 1.0, inc=sve)

        x_add_done = [0] * NT

        def emit_ln(x_ready_fn, tiles=None, dst_lnT=None, tile_cb=None,
                    stats_save=None, stats_use=None, lnbf_guard=0):
            if tiles is None:
                tiles = list(range(NT))
            A.mark()
            lnbf = [A.b(2048), A.b(2048)]
            if dst_lnT is None:
                lntile = [A.b(2048), A.b(2048)]
                cb_done = []
            else:
                lnT3 = dst_lnT.rearrange("p (d t) -> p d t", d=ND)
            evac_done, apply_done, tr_done = [], [], []
            for i, t in enumerate(tiles):
                Xt = X[:, 2048 * t:2048 * (t + 1)]
                sb8 = 6 * (i % 2)
                c_sum = SMALL[:, sb8:sb8 + 1]
                c_negm = SMALL[:, sb8 + 1:sb8 + 2]
                c_ssq = SMALL[:, sb8 + 2:sb8 + 3]
                c_std = SMALL[:, sb8 + 3:sb8 + 4]
                c_rstd = SMALL[:, sb8 + 4:sb8 + 5]
                c_negmr = SMALL[:, sb8 + 5:sb8 + 6]
                slot = lnbf[i % 2]
                x_ready_fn(t)
                if stats_use is not None:
                    if i >= 2:
                        W("scalar", spe, tr_done[i - 2])
                    else:
                        W("scalar", spe, lnbf_guard)
                    s3 = E("scalar", "activation", slot, Xt, AF.Identity,
                           bias=stats_use[0][:, t:t + 1],
                           scale=stats_use[1][:, t:t + 1], inc=sse)
                    apply_done.append(s3)
                    pb = (0, 1) if i % 2 == 0 else (2, 3)
                    W("tensor", sse, s3)
                    P.ps_pre_write("tensor", pb)
                    psb = bankb(pb[0], 2)
                    for j in range(ND):
                        kw = dict(inc=spe) if j == ND - 1 else {}
                        E("tensor", "transpose", psb[:, 128 * j:128 * (j + 1)],
                          slot[:, 128 * j:128 * (j + 1)], IDB[:], **kw)
                    tr_done.append(spe.n)
                    P.ps_post_write(pb, spe, spe.n)
                    P.ps_pre_read("vector", pb)
                    ve = E("vector", "tensor_copy",
                           lnT3[:, :, 128 * i:128 * (i + 1)],
                           psb.rearrange("p (j c) -> p j c", j=ND), inc=sve)
                    evac_done.append(ve)
                    P.ps_post_read(pb, sve, ve)
                    continue
                if i >= 2:
                    W("vector", sse, apply_done[i - 2])
                v1 = E("vector", "reduce_sum", c_sum, Xt, axis=AX.X, inc=sve)
                W("vector", sve, v1)
                v2 = E("vector", "tensor_scalar", c_negm, c_sum, -1.0 / D, None,
                       op0=ALU.mult, inc=sve)
                W("scalar", sve, v2)
                if i >= 2:
                    W("scalar", spe, tr_done[i - 2])
                else:
                    W("scalar", spe, lnbf_guard)
                s1 = E("scalar", "activation", slot, Xt, AF.Square,
                       bias=c_negm, scale=f32(1.0), accum_out=c_ssq, inc=sse)
                W("scalar", sse, s1)
                s2 = E("scalar", "activation", c_std, c_ssq, AF.Sqrt,
                       bias=f32(EPS), scale=f32(1.0 / D), inc=sse)
                W("vector", sse, s2)
                v3 = E("vector", "reciprocal", c_rstd, c_std, inc=sve)
                W("vector", sve, v3)
                v4 = E("vector", "tensor_mul", c_negmr, c_negm, c_rstd, inc=sve)
                if stats_save is not None:
                    W("vector", sve, v4)
                    E("vector", "tensor_copy", stats_save[0][:, t:t + 1], c_negmr,
                      inc=sve)
                    v4b = E("vector", "tensor_copy", stats_save[1][:, t:t + 1],
                            c_rstd, inc=sve)
                W("scalar", sve, v4)
                s3 = E("scalar", "activation", slot, Xt, AF.Identity,
                       bias=c_negmr, scale=c_rstd, inc=sse)
                apply_done.append(s3)
                pb = (0, 1) if i % 2 == 0 else (2, 3)
                W("tensor", sse, s3)
                P.ps_pre_write("tensor", pb)
                psb = bankb(pb[0], 2)
                for j in range(ND):
                    kw = dict(inc=spe) if j == ND - 1 else {}
                    E("tensor", "transpose", psb[:, 128 * j:128 * (j + 1)],
                      slot[:, 128 * j:128 * (j + 1)], IDB[:], **kw)
                tr_done.append(spe.n)
                P.ps_post_write(pb, spe, spe.n)
                P.ps_pre_read("vector", pb)
                if dst_lnT is not None:
                    ve = E("vector", "tensor_copy",
                           lnT3[:, :, 128 * i:128 * (i + 1)],
                           psb.rearrange("p (j c) -> p j c", j=ND), inc=sve)
                else:
                    if i >= 2:
                        W("vector", spe, cb_done[i - 2])
                    ve = E("vector", "tensor_copy", lntile[i % 2][:], psb, inc=sve)
                evac_done.append(ve)
                P.ps_post_read(pb, sve, ve)
                if dst_lnT is None:
                    cb_done.append(tile_cb(i, t, lntile[i % 2], ve))
            A.release()
            emit_ln.last_tr = tr_done[-1]
            return evac_done[-1]

        def emit_attn_a():
            A.mark()
            basisT = A.b(4 * 1024)
            wTqk = A.b(2048)
            wTvo = A.b(2048)

            A.mark()
            bsb = A.b(2048)
            wstg = A.b(1024)
            pa = A.f(1024)
            am = A.f(1024)
            cs = A.f(1024)
            prev = {}
            W("tensor", sl, p0_loads)
            for pi, p in enumerate("qkvo"):
                if pi > 0:
                    W("gpsimd", spe, prev["btr"])
                    W("gpsimd", sse, prev["sin"])
                    W("gpsimd", sve, prev["wmul"])
                DMA("gpsimd", bsb[0:64, :], inp[f"basis_{p}"], sg)
                DMA("gpsimd", pa.rearrange("p (b h) -> p b h", b=ND), inp[f"phase_{p}"].rearrange("(b p) h -> p b h", p=128), sg)
                g3 = DMA("gpsimd", am.rearrange("p (b h) -> p b h", b=ND), inp[f"amp_{p}"].rearrange("(b p) h -> p b h", p=128), sg)
                W("tensor", sg, g3)
                W("scalar", sg, g3)
                pb = (0,)
                P.ps_pre_write("tensor", pb)
                for j in range(ND):
                    kw = dict(inc=spe) if j == ND - 1 else {}
                    E("tensor", "transpose", bankb(0, 1)[:, 64 * j:64 * (j + 1)],
                      bsb[0:64, 128 * j:128 * (j + 1)], IDB[0:64, 0:64], **kw)
                prev["btr"] = spe.n
                P.ps_post_write(pb, spe, spe.n)
                P.ps_pre_read("vector", pb)
                bev = E("vector", "tensor_copy",
                        basisT[:, 1024 * pi:1024 * (pi + 1)].rearrange("p (j c) -> p j c", j=ND),
                        bankb(0, 1).rearrange("p (j c) -> p j c", j=ND), inc=sve)
                P.ps_post_read(pb, sve, bev)

                if pi > 0:
                    W("scalar", sve, prev["wmul"])
                prev["sin"] = E("scalar", "activation", cs, pa, AF.Sin,
                                bias=f32(math.pi / 2), scale=f32(1.0), inc=sse)
                W("vector", sse, prev["sin"])
                if pi > 0:
                    W("vector", spe, prev["wtr"])
                prev["wmul"] = E("vector", "tensor_mul", wstg, cs, am, inc=sve)
                W("tensor", sve, prev["wmul"])
                pb2 = (2, 3)
                P.ps_pre_write("tensor", pb2)
                for j in range(ND):
                    kw = dict(inc=spe) if j == ND - 1 else {}
                    E("tensor", "transpose", bankb(2, 2)[0:64, 128 * j:128 * (j + 1)],
                      wstg[:, 64 * j:64 * (j + 1)], IDB[:], **kw)
                prev["wtr"] = spe.n
                P.ps_post_write(pb2, spe, spe.n)
                dstw = wTqk if pi < 2 else wTvo
                rows = slice(0, 64) if pi % 2 == 0 else slice(64, 128)
                P.ps_pre_read("vector", pb2)
                wev = E("vector", "tensor_copy",
                        dstw[rows, :].rearrange("p (j c) -> p j c", j=ND),
                        bankb(2, 2)[0:64, :].rearrange("p (j c) -> p j c", j=ND), inc=sve)
                P.ps_post_read(pb2, sve, wev)
            A.release()
            prep_done = sve.n

            resQK = A.b(2048)
            resV = A.b(2048)

            def x_ready(t):
                W("vector", sl, p0_loads)
                W("scalar", sl, p0_loads)

            def res_cb(i, t, tile, evac_cnt):
                last = None
                for pi in range(3):
                    pb = 4 + (pi % 2)
                    W("tensor", sve, evac_cnt)
                    if i == 0 and pi == 0:
                        W("tensor", sve, prep_done)
                    P.ps_pre_write("tensor", (pb,))
                    for d in range(ND):
                        kw = dict(inc=spe) if d == ND - 1 else {}
                        E("tensor", "matmul", bank(pb)[0:64, 0:128],
                          basisT[:, 1024 * pi + 64 * d:1024 * pi + 64 * (d + 1)],
                          tile[:, 128 * d:128 * (d + 1)],
                          start=(d == 0), stop=(d == ND - 1), **kw)
                    last = spe.n
                    P.ps_post_write((pb,), spe, last)
                    dst = [resQK[0:64, :], resQK[64:128, :], resV[0:64, :]][pi]
                    P.ps_pre_read("vector", (pb,))
                    ev = E("vector", "tensor_copy", dst[:, 128 * t:128 * (t + 1)],
                           bank(pb)[0:64, 0:128], inc=sve)
                    P.ps_post_read((pb,), sve, ev)
                return last

            emit_ln(x_ready, tile_cb=res_cb)
            res_done = sve.n

            resOa = A.f(2048)
            qTh = A.b(2048)
            kTh = A.b(2048)
            vh = A.b(2048)
            A.mark()
            probs0_off = A.off
            probs = [A.b(2048), A.b(2048), A.b(2048)]
            probsT = A.b(ND * 512)
            outTh = [A.b(512), A.b(512)]
            rz = E("vector", "memset", resOa, 0.0, inc=sve)

            outT_lastread = [spe.n, spe.n]
            probs_T_done = {}
            mx_lastread = {}
            bias_lastread = {}
            resO_last = rz

            for h in range(NHEAD):
                for nm, dst in (("q", qTh), ("k", kTh)):
                    rw = slice(0, 64) if nm == "q" else slice(64, 128)
                    P.ps_pre_write("tensor", (0, 1, 2, 3))
                    W("tensor", sve, res_done)
                    for c in range(4):
                        E("tensor", "matmul", bank(c), wTqk[rw, 128 * h:128 * (h + 1)],
                          resQK[rw, 512 * c:512 * (c + 1)], start=True, stop=True,
                          inc=(spe if c == 3 else None))
                    P.ps_post_write((0, 1, 2, 3), spe, spe.n)
                    P.ps_pre_read("vector", (0, 1, 2, 3))
                    qe = E("vector", "tensor_copy", dst[:], bank(0, 4), inc=sve)
                    P.ps_post_read((0, 1, 2, 3), sve, qe)
                P.ps_pre_write("tensor", (0, 1, 2, 3))
                for tg in range(NT):
                    E("tensor", "matmul", bank(tg // 4)[:, 128 * (tg % 4):128 * (tg % 4 + 1)],
                      resV[0:64, 128 * tg:128 * (tg + 1)], wTvo[0:64, 128 * h:128 * (h + 1)],
                      start=True, stop=True, inc=(spe if tg == NT - 1 else None))
                P.ps_post_write((0, 1, 2, 3), spe, spe.n)
                P.ps_pre_read("vector", (0, 1, 2, 3))
                ve_v = E("vector", "tensor_copy", vh[:], bank(0, 4), inc=sve)
                P.ps_post_read((0, 1, 2, 3), sve, ve_v)
                qkv_done = ve_v

                def head_part(qi):
                    par = qi % 2
                    gqi = 16 * h + qi
                    pslot = probs[gqi % 3]
                    slot_free = probs_T_done.get(gqi - 3, 0)
                    mx = SMALL[:, 16 + 6 * par:16 + 6 * par + 4]
                    cbias = SMALL[:, 20 + 6 * par:21 + 6 * par]
                    crq = SMALL[:, 21 + 6 * par:22 + 6 * par]
                    for c in range(4):
                        if c == 0:
                            W("tensor", sve, qkv_done)
                        P.ps_pre_write("tensor", (4 + c,))
                        mm = E("tensor", "matmul", bank(4 + c),
                               qTh[:, 128 * qi:128 * (qi + 1)],
                               kTh[:, 512 * c:512 * (c + 1)],
                               start=True, stop=True, inc=spe)
                        P.ps_post_write((4 + c,), spe, mm)
                    W("vector", sve, mx_lastread.get(par, 0))
                    P.ps_pre_read("vector", (4, 5))
                    vm0 = E("vector", "reduce_max", mx[:, 0:1], bank(4, 2),
                            axis=AX.X, inc=sve)
                    P.ps_post_read((4, 5), sve, vm0)
                    P.ps_pre_read("vector", (6, 7))
                    vm1 = E("vector", "reduce_max", mx[:, 1:2], bank(6, 2),
                            axis=AX.X, inc=sve)
                    P.ps_post_read((6, 7), sve, vm1)
                    W("vector", sve, vm1)
                    W("vector", sse, bias_lastread.get(par, 0))
                    vt = E("vector", "tensor_max", cbias, mx[:, 0:1], mx[:, 1:2],
                           inc=sve)
                    W("vector", sve, vt)
                    vbs = E("vector", "tensor_scalar", cbias, cbias, float(-SCALE),
                            None, op0=ALU.mult, inc=sve)
                    return par, pslot, gqi, slot_free, mx, cbias, crq, vbs

                def exp_part(state):
                    par, pslot, gqi, slot_free, mx, cbias, crq, vbs = state
                    last_exp = 0
                    for hf in range(2):
                        W("scalar", sve, vbs)
                        if hf == 0:
                            W("scalar", spe, slot_free)
                        P.ps_pre_read("scalar", (4 + 2 * hf, 5 + 2 * hf))
                        le = E("scalar", "activation",
                               pslot[:, 1024 * hf:1024 * (hf + 1)],
                               bank(4 + 2 * hf, 2), AF.Exp, bias=cbias,
                               scale=SCALE, accum_out=mx[:, 2 + hf:3 + hf], inc=sse)
                        last_exp = le
                        P.ps_post_read((4 + 2 * hf, 5 + 2 * hf), sse, le)
                    bias_lastread[par] = last_exp
                    return last_exp

                def finish_a(state, last_exp):
                    par, pslot, gqi, slot_free, mx, cbias, crq, vbs = state
                    W("vector", sse, last_exp)
                    v1 = E("vector", "tensor_add", crq, mx[:, 2:3], mx[:, 3:4],
                           inc=sve)
                    mx_lastread[par] = v1
                    W("vector", sve, v1)
                    v2 = E("vector", "reciprocal", crq, crq, inc=sve)
                    return v2

                def finish_b(state, v2):
                    par, pslot, gqi, slot_free, mx, cbias, crq, vbs = state
                    W("vector", sve, v2)
                    v3 = E("vector", "tensor_scalar", pslot[:], pslot[:], crq,
                           None, op0=ALU.mult, inc=sve)
                    return v3

                def tail_part(qi, v3):
                    nonlocal resO_last
                    qq = qi % 4
                    g = qi // 4
                    gqi = 16 * h + qi
                    pslot = probs[gqi % 3]
                    W("tensor", sve, v3)
                    P.ps_pre_write("tensor", (2, 3))
                    ptb = bankb(2, 2)
                    for j in range(ND):
                        kw = dict(inc=spe) if j == ND - 1 else {}
                        E("tensor", "transpose", ptb[:, 128 * j:128 * (j + 1)],
                          pslot[:, 128 * j:128 * (j + 1)], IDB[:], **kw)
                    probs_T_done[gqi] = spe.n
                    P.ps_post_write((2, 3), spe, spe.n)
                    pT3 = probsT.rearrange("p (k q) -> p k q", k=ND)
                    ptb3 = ptb.rearrange("p (j c) -> p j c", j=ND)
                    P.ps_pre_read("vector", (2,))
                    pev = E("vector", "tensor_copy",
                            pT3[:, 0:8, 128 * qq:128 * (qq + 1)], ptb3[:, 0:8, :],
                            inc=sve)
                    P.ps_post_read((2,), sve, pev)
                    P.ps_pre_read("scalar", (3,))
                    pev2 = E("scalar", "activation",
                             pT3[:, 8:16, 128 * qq:128 * (qq + 1)], ptb3[:, 8:16, :],
                             AF.Identity, bias=f32(0.0), scale=f32(1.0), inc=sse)
                    P.ps_post_read((3,), sse, pev2)
                    if qq == 3:
                        W("tensor", sve, pev)
                        W("tensor", sse, pev2)
                        P.ps_pre_write("tensor", (0,))
                        for k in range(NT):
                            E("tensor", "matmul", bank(0), vh[:, 128 * k:128 * (k + 1)],
                              probsT[:, 512 * k:512 * (k + 1)],
                              start=(k == 0), stop=(k == NT - 1),
                              inc=(spe if k == NT - 1 else None))
                        P.ps_post_write((0,), spe, spe.n)
                        oslot = outTh[g % 2]
                        W("scalar", spe, outT_lastread[g % 2])
                        P.ps_pre_read("scalar", (0,))
                        oev = E("scalar", "activation", oslot[:], bank(0), AF.Identity,
                                bias=f32(0.0), scale=f32(1.0), inc=sse)
                        P.ps_post_read((0,), sse, oev)
                        W("tensor", sse, oev)
                        P.ps_pre_write("tensor", (1,))
                        ro = E("tensor", "matmul", bank(1)[0:64, :],
                               basisT[:, 3 * 1024 + 64 * h:3 * 1024 + 64 * (h + 1)],
                               oslot[:], start=True, stop=True, inc=spe)
                        outT_lastread[g % 2] = ro
                        P.ps_post_write((1,), spe, ro)
                        W("vector", sve, resO_last)
                        P.ps_pre_read("vector", (1,))
                        ra = E("vector", "tensor_add",
                               resOa[0:64, 512 * g:512 * (g + 1)],
                               resOa[0:64, 512 * g:512 * (g + 1)],
                               bank(1)[0:64, :], inc=sve)
                        resO_last = ra
                        P.ps_post_read((1,), sve, ra)

                pending = None
                for qi in range(16):
                    state = head_part(qi)
                    if pending is not None:
                        tail_part(*pending)
                    le = exp_part(state)
                    v2 = finish_a(state, le)
                    v3 = finish_b(state, v2)
                    pending = (qi, v3)
                tail_part(*pending)

            resOb = probs[0]
            W("vector", sve, resO_last)
            W("vector", spe, spe.n)
            rb = E("vector", "tensor_copy", resOb[64:128, 0:2048], resOa[0:64, :], inc=sve)
            for t in range(NT):
                for c in range(4):
                    pb = 2 + ((4 * t + c) % 2)
                    W("tensor", sve, rb)
                    P.ps_pre_write("tensor", (pb,))
                    mm = E("tensor", "matmul", bank(pb),
                           resOb[64:128, 128 * t:128 * (t + 1)],
                           wTvo[64:128, 512 * c:512 * (c + 1)],
                           start=True, stop=True, inc=spe)
                    P.ps_post_write((pb,), spe, mm)
                    W("vector", spe, mm)
                    if x_add_done[t]:
                        W("vector", sve, x_add_done[t])
                    P.ps_pre_read("vector", (pb,))
                    xs = X[:, 2048 * t + 512 * c:2048 * t + 512 * (c + 1)]
                    va = E("vector", "tensor_add", xs, xs, bank(pb), inc=sve)
                    x_add_done[t] = va
                    P.ps_post_read((pb,), sve, va)
            A.release()
            A.release()

        sgw = [Sem(ctx.enter_context(nc.semaphore(f"sgw{i}")), "swdge") for i in range(2)]
        sgu = [Sem(ctx.enter_context(nc.semaphore(f"sgu{i}")), "swdge") for i in range(2)]
        sgd = Sem(ctx.enter_context(nc.semaphore("sgd")), "swdge")
        sgr = Sem(ctx.enter_context(nc.semaphore("sgr")), "swdge")
        sth = [Sem(ctx.enter_context(nc.semaphore(f"sth{i}")), "hwdge") for i in range(2)]
        shr = [Sem(ctx.enter_context(nc.semaphore(f"shr{i}")), "hwdge") for i in range(2)]
        all_sems.extend([sg, sl, st, spe, sve, sse, spo, sgr, sgd] + sgw + sgu + sth + shr)

        def unit_list():
            us = [("s", NE * DE, NJ_S, inp["Ws_gate"], inp["Ws_up"], inp["Ws_down"])]
            for e in range(NE):
                us.append((e, e * DE, NJ_E, inp["We_gate"][e], inp["We_up"][e],
                           inp["We_down"][e]))
            return us

        ffn_state = {"wr_loaded": 0, "comb_lastread": 0, "d_read_done": (0, 0)}

        def emit_ffn(call):
            barrier()
            A.mark()
            RT = A.f(128)
            RTP = A.f(128)
            wrb = A.b(128)
            g0 = DMA("gpsimd", wrb.rearrange("p (d r) -> p d r", d=ND),
                     inp["W_router"].rearrange("(d p) r -> p d r", p=128), sgr)

            units = unit_list()
            gH_done = []
            h_written = []

            for half in range(2):
                if half == 1:
                    barrier()
                tiles = list(range(8 * half, 8 * half + 8))
                A.mark()
                lnT_h = A.b(ND * 1024)

                def x_ready(t):
                    W("vector", sve, x_add_done[t])
                    W("scalar", sve, x_add_done[t])

                ln_done = emit_ln(x_ready, tiles=tiles, dst_lnT=lnT_h)

                W("tensor", sgr, sgr.n)
                rt_ev = None
                A.mark()
                rstage = A.f(1024)
                for c2 in range(2):
                    P.ps_pre_write("tensor", (0,))
                    W("tensor", sve, ln_done)
                    for d in range(ND):
                        E("tensor", "matmul", bank(0)[0:8, :],
                          wrb[:, 8 * d:8 * (d + 1)],
                          lnT_h[:, 1024 * d + 512 * c2:1024 * d + 512 * (c2 + 1)],
                          start=(d == 0), stop=(d == ND - 1),
                          inc=(spe if d == ND - 1 else None))
                    P.ps_post_write((0,), spe, spe.n)
                    P.ps_pre_read("vector", (0,))
                    rev = E("vector", "tensor_copy", rstage[0:8, 512 * c2:512 * (c2 + 1)],
                            bank(0)[0:8, :], inc=sve)
                    P.ps_post_read((0,), sve, rev)
                W("tensor", sve, sve.n)
                W("tensor", sl, p0_loads)
                P.ps_pre_write("tensor", (1,))
                for i8 in range(8):
                    E("tensor", "transpose", bank(1)[:, 8 * i8:8 * (i8 + 1)],
                      rstage[0:8, 128 * i8:128 * (i8 + 1)], IDF[0:8, 0:8],
                      inc=(spe if i8 == 7 else None))
                P.ps_post_write((1,), spe, spe.n)
                P.ps_pre_read("vector", (1,))
                rt_ev = E("vector", "tensor_copy", RT[:, 64 * half:64 * (half + 1)],
                          bank(1)[:, 0:64], inc=sve)
                P.ps_post_read((1,), sve, rt_ev)
                A.release()

                cNM = SMALL[:, 28:29]
                cRS = SMALL[:, 29:30]
                cM1 = SMALL[:, 30:31]
                cM2 = SMALL[:, 31:32]
                tEQ = SMALL[:, 32:40]
                tMK = SMALL[:, 40:48]
                for i, t in enumerate(tiles):
                    rts = RT[:, 8 * (t % 8) + 64 * half:8 * (t % 8) + 64 * half + 8]
                    rtp = RTP[:, 8 * (t % 8) + 64 * half:8 * (t % 8) + 64 * half + 8]
                    W("vector", sve, rt_ev)
                    if i > 0:
                        W("vector", sve, comb_w)
                        W("vector", sse, exp_done)
                    v1 = E("vector", "reduce_max", cNM, rts, axis=AX.X,
                           negate=True, inc=sve)
                    W("scalar", sve, v1)
                    exp_done = E("scalar", "activation", rtp, rts, AF.Exp,
                                 bias=cNM, scale=f32(1.0), accum_out=cRS, inc=sse)
                    W("vector", sse, exp_done)
                    v2 = E("vector", "reciprocal", cRS, cRS, inc=sve)
                    W("vector", sve, v2)
                    v3 = E("vector", "tensor_scalar", rtp, rtp, cRS, None,
                           op0=ALU.mult, inc=sve)
                    W("vector", sve, v3)
                    v4 = E("vector", "reduce_max", cM1, rtp, axis=AX.X, inc=sve)
                    W("vector", sve, v4)
                    v5 = E("vector", "tensor_scalar", tEQ, rtp, cM1, None,
                           op0=ALU.is_equal, inc=sve)
                    W("vector", sve, v5)
                    v6 = E("vector", "scalar_tensor_tensor", tMK, tEQ, -4.0, rtp,
                           op0=ALU.mult, op1=ALU.add, inc=sve)
                    W("vector", sve, v6)
                    v7 = E("vector", "reduce_max", cM2, tMK, axis=AX.X, inc=sve)
                    W("vector", sve, v7)
                    v8 = E("vector", "tensor_scalar", tEQ, rtp, cM2, None,
                           op0=ALU.is_ge, inc=sve)
                    W("vector", sve, v8)
                    W("vector", sve, ffn_state["comb_lastread"])
                    comb_w = E("vector", "tensor_mul", COMB[:, 8 * t:8 * (t + 1)],
                               tEQ, rtp, inc=sve)
                comb_done = sve.n

                wgb = [A.b(2048), A.b(2048)]
                wub = [A.b(2048), A.b(2048)]
                sgl = [A.f(1024), A.f(1024)]
                hjb = [A.b(1024), A.b(1024)]
                mm_done = {}
                silu_done = {}
                hmul_done = {}
                hdma_done = {}
                jj = 0
                W("gpsimd", sse, sse.n)
                W("gpsimd", sve, sve.n)
                W("gpsimd", spe, spe.n)
                if call > 0:
                    W("sync", shr[0], ffn_state["d_read_done"][0])
                    W("sync", shr[1], ffn_state["d_read_done"][1])
                for (uname, row0, nj, gsrc, usrc, dsrc) in units:
                    for j in range(nj):
                        sp = jj % 2
                        W("gpsimd", spe, mm_done.get(("g", sp), 0))
                        for dh in range(2):
                            DMA("gpsimd",
                                wgb[sp].rearrange("p (d j) -> p d j", d=ND)[:, 8 * dh:8 * (dh + 1), :],
                                gsrc[1024 * dh:1024 * (dh + 1), 128 * j:128 * (j + 1)].rearrange("(d p) j -> p d j", p=128),
                                sgw[sp])
                        W("gpsimd", spe, mm_done.get(("u", sp), 0))
                        for dh in range(2):
                            DMA("gpsimd",
                                wub[sp].rearrange("p (d j) -> p d j", d=ND)[:, 8 * dh:8 * (dh + 1), :],
                                usrc[1024 * dh:1024 * (dh + 1), 128 * j:128 * (j + 1)].rearrange("(d p) j -> p d j", p=128),
                                sgu[sp])
                        gb = (0, 1) if sp == 0 else (4, 5)
                        ub = (2, 3) if sp == 0 else (6, 7)
                        W("tensor", sgw[sp], sgw[sp].n)
                        W("tensor", sve, ln_done)
                        P.ps_pre_write("tensor", gb)
                        for c2 in range(2):
                            for d in range(ND):
                                E("tensor", "matmul", bank(gb[c2]),
                                  wgb[sp][:, 128 * d:128 * (d + 1)],
                                  lnT_h[:, 1024 * d + 512 * c2:1024 * d + 512 * (c2 + 1)],
                                  start=(d == 0), stop=(d == ND - 1),
                                  inc=(spe if (c2 == 1 and d == ND - 1) else None))
                        mm_done[("g", sp)] = spe.n
                        P.ps_post_write(gb, spe, spe.n)
                        W("tensor", sgu[sp], sgu[sp].n)
                        P.ps_pre_write("tensor", ub)
                        for c2 in range(2):
                            for d in range(ND):
                                E("tensor", "matmul", bank(ub[c2]),
                                  wub[sp][:, 128 * d:128 * (d + 1)],
                                  lnT_h[:, 1024 * d + 512 * c2:1024 * d + 512 * (c2 + 1)],
                                  start=(d == 0), stop=(d == ND - 1),
                                  inc=(spe if (c2 == 1 and d == ND - 1) else None))
                        mm_done[("u", sp)] = spe.n
                        P.ps_post_write(ub, spe, spe.n)
                        W("scalar", sve, hmul_done.get(sp, 0))
                        P.ps_pre_read("scalar", gb)
                        sd = E("scalar", "activation", sgl[sp], bank(gb[0], 2),
                               AF.Silu, bias=f32(0.0), scale=f32(1.0), inc=sse)
                        silu_done[sp] = sd
                        P.ps_post_read(gb, sse, sd)
                        W("vector", sse, sd)
                        W("vector", sth[sp], hdma_done.get(sp, 0))
                        P.ps_pre_read("vector", ub)
                        hm = E("vector", "tensor_mul", hjb[sp], sgl[sp],
                               bank(ub[0], 2), inc=sve)
                        hmul_done[sp] = hm
                        P.ps_post_read(ub, sve, hm)
                        W("sync", sve, hm)
                        hdma_done[sp] = DMA("sync",
                                            h_scr[row0 + 128 * j:row0 + 128 * (j + 1),
                                                  1024 * half:1024 * (half + 1)],
                                            hjb[sp], sth[sp])
                        jj += 1
                gH_done.append(spe.n)
                h_written.append((sth[0].n, sth[1].n))
                A.release()

            def emit_sub(uname, row0, nj, dsrc, oh, ow, wd_ld, sp):
                shared = uname == "s"
                nb = ow // 512
                for ti, t in enumerate(range(NT)):
                    hsp = ti % 2
                    W("sync", spe, hb_used.get(hsp, 0))
                    rb = DMA("sync",
                             hb_e[hsp].rearrange("p (j c) -> p j c", j=NJ_S)[:, 0:nj, :],
                             h_scr[row0:row0 + nj * 128, 128 * t:128 * (t + 1)].rearrange("(j p) c -> p j c", p=128),
                             shr[hsp])
                    ybs = ((0, 1) if ti % 2 == 0 else (4, 5))[:nb]
                    W("tensor", shr[hsp], rb)
                    W("tensor", sgdp[sp], wd_ld)
                    P.ps_pre_write("tensor", ybs)
                    for j in range(nj):
                        for cc in range(nb):
                            E("tensor", "matmul", bank(ybs[cc]),
                              hb_e[hsp][:, 128 * j:128 * (j + 1)],
                              wd_sb[sp][:, ow * j + 512 * cc:ow * j + 512 * (cc + 1)],
                              start=(j == 0), stop=(j == nj - 1),
                              skip_group_check=True,
                              inc=(spe if (j == nj - 1 and cc == nb - 1) else None))
                    hb_used[hsp] = spe.n
                    P.ps_post_write(ybs, spe, spe.n)
                    W("vector", spe, spe.n)
                    W("vector", sve, x_add_done[t])
                    P.ps_pre_read("vector", ybs)
                    xsl = X[:, 2048 * t + ow * oh:2048 * t + ow * oh + ow]
                    if shared:
                        va = E("vector", "tensor_add", xsl, xsl, bank(ybs[0], nb),
                               inc=sve)
                    else:
                        va = E("vector", "scalar_tensor_tensor", xsl,
                               bank(ybs[0], nb), COMB[:, 8 * t + uname:8 * t + uname + 1],
                               xsl, op0=ALU.mult, op1=ALU.add, inc=sve)
                    x_add_done[t] = va
                    P.ps_post_read(ybs, sve, va)
                wd_used[sp] = spe.n

            barrier()
            A.mark()
            wd_sb = [A.b(NJ_E * 1024), A.b(NJ_E * 1024)]
            hb_e = [A.b(NJ_S * 128), A.b(NJ_S * 128)]
            W("sync", sth[0], sth[0].n)
            W("sync", sth[1], sth[1].n)
            wd_used = {}
            hb_used = {}
            sgdp = [sgw[0], sgw[1]]
            subunits = []
            for (uname, row0, nj, gsrc, usrc, dsrc) in units:
                ow = 512 if uname == "s" else 1024
                for oh in range(2048 // ow):
                    subunits.append((uname, row0, nj, dsrc, oh, ow))
            for si, su in enumerate(subunits):
                uname, row0, nj, dsrc, oh, ow = su
                sp = si % 2
                W("gpsimd", spe, wd_used.get(sp, 0))
                for j in range(nj):
                    DMA("gpsimd", wd_sb[sp][:, ow * j:ow * (j + 1)],
                        dsrc[128 * j:128 * (j + 1), ow * oh:ow * (oh + 1)],
                        sgdp[sp])
                subunits[si] = (uname, row0, nj, dsrc, oh, ow, sgdp[sp].n, sp)
                if si >= 1:
                    emit_sub(*subunits[si - 1])
            emit_sub(*subunits[-1])
            ffn_state["comb_lastread"] = sve.n
            ffn_state["d_read_done"] = (shr[0].n, shr[1].n)
            A.release()
            A.release()

        swq = [Sem(ctx.enter_context(nc.semaphore(f"swq{i}")), "hwdge") for i in range(2)]
        swk = Sem(ctx.enter_context(nc.semaphore("swk")), "hwdge")
        stw = [Sem(ctx.enter_context(nc.semaphore(f"stw{i}")), "hwdge") for i in range(2)]
        sgt = [Sem(ctx.enter_context(nc.semaphore(f"sgt{i}")), "swdge") for i in range(2)]
        sot = [Sem(ctx.enter_context(nc.semaphore(f"sot{i}")), "hwdge") for i in range(2)]
        all_sems.extend(swq + [swk] + stw + sgt + sot)

        def emit_wtprep():
            barrier()
            A.mark()
            wtmp = [A.b(2048), A.b(2048)]
            wstage = [A.b(2048), A.b(2048)]
            blocks = []
            for ob in range(16):
                blocks.append((inp["Wq"], ob, 128 * ob))
            for ob in range(4):
                blocks.append((inp["Wk"], ob, 2048 + 128 * ob))
            for ob in range(4):
                blocks.append((inp["Wv"], ob, 2560 + 128 * ob))
            for ob in range(16):
                blocks.append((inp["Wo"], ob, 3072 + 128 * ob))
            tr_done = {}
            ev_done = {}
            dma_out = {}
            for i, (src, ob, colb) in enumerate(blocks):
                sp = i % 2
                if i >= 2:
                    W("gpsimd", spe, tr_done[i - 2])
                DMA("gpsimd", wtmp[sp], src[128 * ob:128 * (ob + 1), :], sgt[sp])
                W("tensor", sgt[sp], sgt[sp].n)
                pb = (0, 1) if sp == 0 else (2, 3)
                P.ps_pre_write("tensor", pb)
                psb = bankb(pb[0], 2)
                for j in range(ND):
                    kw = dict(inc=spe) if j == ND - 1 else {}
                    E("tensor", "transpose", psb[:, 128 * j:128 * (j + 1)],
                      wtmp[sp][:, 128 * j:128 * (j + 1)], IDB[:], **kw)
                tr_done[i] = spe.n
                P.ps_post_write(pb, spe, spe.n)
                if i >= 2:
                    W("vector", stw[sp], dma_out[i - 2])
                P.ps_pre_read("vector", pb)
                ev = E("vector", "tensor_copy", wstage[sp][:], psb, inc=sve)
                ev_done[i] = ev
                P.ps_post_read(pb, sve, ev)
                W("sync", sve, ev)
                dma_out[i] = DMA("sync",
                                 wT_scr[:, colb:colb + 128].rearrange("(j p) c -> p j c", p=128),
                                 wstage[sp].rearrange("p (j c) -> p j c", j=ND), stw[sp])
            bq_tm = SMALL[:, 80:96]
            bk_tm = SMALL[:, 96:100]
            bv_tm = SMALL[:, 100:104]
            bq_sw = SMALL[:, 12:28]
            bk_sw = SMALL[:, 116:120]
            for hh in range(16):
                DMA("sync", bq_tm[:, hh:hh + 1], inp["bq"][128 * hh:128 * (hh + 1)].unsqueeze(1), sl)
                DMA("sync", bq_sw[0:64, hh:hh + 1], inp["bq"][128 * hh + 64:128 * (hh + 1)].unsqueeze(1), sl)
                DMA("sync", bq_sw[64:128, hh:hh + 1], inp["bq"][128 * hh:128 * hh + 64].unsqueeze(1), sl)
            for gg in range(4):
                DMA("sync", bk_tm[:, gg:gg + 1], inp["bk"][128 * gg:128 * (gg + 1)].unsqueeze(1), sl)
                DMA("sync", bk_sw[0:64, gg:gg + 1], inp["bk"][128 * gg + 64:128 * (gg + 1)].unsqueeze(1), sl)
                DMA("sync", bk_sw[64:128, gg:gg + 1], inp["bk"][128 * gg:128 * gg + 64].unsqueeze(1), sl)
                DMA("sync", bv_tm[:, gg:gg + 1], inp["bv"][128 * gg:128 * (gg + 1)].unsqueeze(1), sl)
            A.release()
            return sl.n

        def emit_attn_b():
            bias_loaded = emit_wtprep()
            negmr_sv = SMALL[:, 48:64]
            rstd_sv = SMALL[:, 64:80]
            bq_tm = SMALL[:, 80:96]
            bk_tm = SMALL[:, 96:100]
            bv_tm = SMALL[:, 100:104]
            bq_sw = SMALL[:, 12:28]
            bk_sw = SMALL[:, 116:120]
            cMx = SMALL[:, 104:108]
            cMxA = SMALL[:, 108:112]
            cMxB = SMALL[:, 112:116]

            for pss in range(4):
                barrier()
                A.mark()
                qT = A.b(4 * 2048)
                kT = A.b(2048)
                vg = A.b(2048)
                A.mark()
                lnchunk = A.b(ND * 512)
                wqb = [A.b(2048), A.b(2048)]
                wkv = A.b(4096)
                tblc = A.f(512)
                tbls = A.f(512)
                tmp1 = A.f(512)
                tmp2 = A.f(512)
                g = pss

                ln_ev = {}
                rope_done = 0
                tbl_used = 0
                wkv_used = 0
                wq_used = {}
                v_ev = 0
                for c in range(4):
                    tiles = list(range(4 * c, 4 * c + 4))

                    def x_ready(t):
                        W("vector", sve, x_add_done[t])
                        W("scalar", sve, x_add_done[t])

                    lg = getattr(emit_ln, "last_tr", 0) if c > 0 else 0
                    if pss == 0:
                        lnd = emit_ln(x_ready, tiles=tiles, dst_lnT=lnchunk,
                                      stats_save=(negmr_sv, rstd_sv), lnbf_guard=lg)
                    else:
                        lnd = emit_ln(x_ready, tiles=tiles, dst_lnT=lnchunk,
                                      stats_use=(negmr_sv, rstd_sv), lnbf_guard=lg)
                    W("sync", sve, rope_done)
                    DMA("sync", tblc, inp["cosq"][:, 512 * c:512 * (c + 1)], sl)
                    DMA("sync", tbls, inp["sinq"][:, 512 * c:512 * (c + 1)], sl)
                    W("sync", spe, wkv_used)
                    DMA("sync", wkv[:, 0:2048].rearrange("p (d j) -> p d j", d=ND)[:, 0:8, :],
                        wT_scr[0:1024, 2048 + 128 * g:2048 + 128 * (g + 1)].rearrange("(d p) c -> p d c", p=128), swk)
                    DMA("sync", wkv[:, 0:2048].rearrange("p (d j) -> p d j", d=ND)[:, 8:16, :],
                        wT_scr[1024:2048, 2048 + 128 * g:2048 + 128 * (g + 1)].rearrange("(d p) c -> p d c", p=128), swk)
                    DMA("sync", wkv[:, 2048:4096].rearrange("p (d j) -> p d j", d=ND)[:, 0:8, :],
                        wT_scr[0:1024, 2560 + 128 * g:2560 + 128 * (g + 1)].rearrange("(d p) c -> p d c", p=128), swk)
                    kvld = DMA("sync", wkv[:, 2048:4096].rearrange("p (d j) -> p d j", d=ND)[:, 8:16, :],
                               wT_scr[1024:2048, 2560 + 128 * g:2560 + 128 * (g + 1)].rearrange("(d p) c -> p d c", p=128), swk)

                    def rope_evac(pbk, bias_col, bias_sw, dst):
                        nonlocal rope_done
                        W("vector", sl, sl.n)
                        W("vector", sl, bias_loaded)
                        W("vector", sve, rope_done)
                        P.ps_pre_read("vector", (pbk,))
                        r1 = E("vector", "scalar_tensor_tensor", tmp1, bank(pbk),
                               bias_col, tblc, op0=ALU.add, op1=ALU.mult, inc=sve)
                        r2 = E("vector", "scalar_tensor_tensor", tmp2[0:64, :],
                               bank(pbk)[64:128, :], bias_sw[0:64, :], tbls[0:64, :],
                               op0=ALU.add, op1=ALU.mult, inc=sve)
                        r3 = E("vector", "scalar_tensor_tensor", tmp2[64:128, :],
                               bank(pbk)[0:64, :], bias_sw[64:128, :], tbls[64:128, :],
                               op0=ALU.add, op1=ALU.mult, inc=sve)
                        W("vector", sve, r3)
                        r4 = E("vector", "tensor_add", dst, tmp1, tmp2, inc=sve)
                        rope_done = r4
                        P.ps_post_read((pbk,), sve, r4)
                        return r4

                    W("tensor", swk, kvld)
                    W("tensor", sve, lnd)
                    pb = 4
                    P.ps_pre_write("tensor", (pb,))
                    for d in range(ND):
                        E("tensor", "matmul", bank(pb),
                          wkv[:, 128 * d:128 * (d + 1)],
                          lnchunk[:, 512 * d:512 * (d + 1)],
                          start=(d == 0), stop=(d == ND - 1),
                          inc=(spe if d == ND - 1 else None))
                    P.ps_post_write((pb,), spe, spe.n)
                    rope_evac(pb, bk_tm[:, g:g + 1], bk_sw[:, g:g + 1], kT[:, 512 * c:512 * (c + 1)])
                    for hl in range(4):
                        h = 4 * pss + hl
                        sp = hl % 2
                        W("sync", spe, wq_used.get(sp, 0))
                        DMA("sync", wqb[sp].rearrange("p (d j) -> p d j", d=ND)[:, 0:8, :],
                            wT_scr[0:1024, 128 * h:128 * (h + 1)].rearrange("(d p) c -> p d c", p=128), swq[sp])
                        qld = DMA("sync", wqb[sp].rearrange("p (d j) -> p d j", d=ND)[:, 8:16, :],
                                  wT_scr[1024:2048, 128 * h:128 * (h + 1)].rearrange("(d p) c -> p d c", p=128), swq[sp])
                        pbq = 5 + sp
                        W("tensor", swq[sp], qld)
                        W("tensor", sve, lnd)
                        P.ps_pre_write("tensor", (pbq,))
                        for d in range(ND):
                            E("tensor", "matmul", bank(pbq),
                              wqb[sp][:, 128 * d:128 * (d + 1)],
                              lnchunk[:, 512 * d:512 * (d + 1)],
                              start=(d == 0), stop=(d == ND - 1),
                              inc=(spe if d == ND - 1 else None))
                        wq_used[sp] = spe.n
                        P.ps_post_write((pbq,), spe, spe.n)
                        rope_evac(pbq, bq_tm[:, h:h + 1], bq_sw[:, h:h + 1],
                                  qT[:, 2048 * hl + 512 * c:2048 * hl + 512 * (c + 1)])
                    W("tensor", swk, kvld)
                    P.ps_pre_write("tensor", (7,))
                    for tt in range(4):
                        for d in range(ND):
                            E("tensor", "matmul", bank(7)[:, 128 * tt:128 * (tt + 1)],
                              lnchunk[:, 512 * d + 128 * tt:512 * d + 128 * (tt + 1)],
                              wkv[:, 2048 + 128 * d:2048 + 128 * (d + 1)],
                              start=(d == 0), stop=(d == ND - 1),
                              inc=(spe if (tt == 3 and d == ND - 1) else None))
                    wkv_used = spe.n
                    P.ps_post_write((7,), spe, spe.n)
                    P.ps_pre_read("vector", (7,))
                    v_ev = E("vector", "tensor_copy", vg[:, 512 * c:512 * (c + 1)],
                             bank(7), inc=sve)
                    P.ps_post_read((7,), sve, v_ev)
                qkv_done = sve.n
                A.release()
                barrier()

                A.mark()
                probs = [A.b(2048), A.b(2048), A.b(2048)]
                probsT = A.b(ND * 512)
                outTh = [A.b(512), A.b(512)]
                probs_T_done = {}
                mx_lastread = {}
                bias_lastread = {}
                rq_lastread = {}
                outT_dma = {}
                pz = 0
                def b_head_part(hl, qi):
                    qTh_ = qT[:, 2048 * hl:2048 * (hl + 1)]
                    par = qi % 2
                    gqi = 64 * pss + 16 * hl + qi
                    pslot = probs[gqi % 3]
                    nvalid = qi + 1
                    wlast = (qi % 4 + 1) * 128
                    nch = qi // 4 + 1
                    slot_free = probs_T_done.get(gqi - 3, 0)
                    mxb = cMxA if par == 0 else cMxB
                    dm = 0
                    for kc in range(nch):
                        wd = 512 if kc < nch - 1 else wlast
                        if kc == 0:
                            W("tensor", sve, qkv_done)
                        P.ps_pre_write("tensor", (4 + kc,))
                        mm = E("tensor", "matmul", bank(4 + kc)[:, 0:wd],
                               qTh_[:, 128 * qi:128 * (qi + 1)],
                               kT[:, 512 * kc:512 * kc + wd],
                               start=True, stop=True, inc=spe)
                        P.ps_post_write((4 + kc,), spe, mm)
                        if kc == nch - 1:
                            W("vector", spe, mm)
                            W("vector", sl, p0_loads)
                            P.ps_pre_read("vector", (4 + kc,))
                            dm = E("vector", "tensor_add",
                                   bank(4 + kc)[:, wd - 128:wd],
                                   bank(4 + kc)[:, wd - 128:wd], MASKN[:], inc=sve)
                        if kc == 0:
                            W("vector", sve, mx_lastread.get(par, 0))
                        if kc == nch - 1:
                            W("vector", sve, dm)
                        P.ps_pre_read("vector", (4 + kc,))
                        vm = E("vector", "reduce_max", mxb[:, kc:kc + 1],
                               bank(4 + kc)[:, 0:wd], axis=AX.X, inc=sve)
                        P.ps_post_read((4 + kc,), sve, vm)
                    W("vector", sve, sve.n)
                    W("vector", sse, bias_lastread.get(par, 0))
                    vt = E("vector", "reduce_max", cMx[:, 2 + par:3 + par],
                           mxb[:, 0:nch], axis=AX.X, inc=sve)
                    W("vector", sve, vt)
                    vb = E("vector", "tensor_scalar", cMx[:, 2 + par:3 + par],
                           cMx[:, 2 + par:3 + par], float(-SCALE), None,
                           op0=ALU.mult, inc=sve)
                    return (hl, qi, par, pslot, gqi, slot_free, mxb, nch, wlast,
                            nvalid, vb)

                def b_exp_part(stt_):
                    hl, qi, par, pslot, gqi, slot_free, mxb, nch, wlast, nvalid, vb = stt_
                    last_exp = 0
                    for kc in range(nch):
                        wd = 512 if kc < nch - 1 else wlast
                        W("scalar", sve, vb)
                        if kc == 0:
                            W("scalar", spe, slot_free)
                        le = E("scalar", "activation",
                               pslot[:, 512 * kc:512 * kc + wd],
                               bank(4 + kc)[:, 0:wd], AF.Exp,
                               bias=cMx[:, 2 + par:3 + par], scale=SCALE,
                               accum_out=mxb[:, kc:kc + 1], inc=sse)
                        last_exp = le
                        P.ps_post_read((4 + kc,), sse, le)
                    bias_lastread[par] = last_exp
                    return last_exp

                def b_finish_a(stt_, last_exp):
                    hl, qi, par, pslot, gqi, slot_free, mxb, nch, wlast, nvalid, vb = stt_
                    W("vector", sse, last_exp)
                    v1 = E("vector", "reduce_sum", cMx[:, par:par + 1],
                           mxb[:, 0:nch], axis=AX.X, inc=sve)
                    mx_lastread[par] = v1
                    W("vector", sve, v1)
                    v2 = E("vector", "reciprocal", cMx[:, par:par + 1],
                           cMx[:, par:par + 1], inc=sve)
                    return v2

                def b_finish_b(stt_, v2):
                    hl, qi, par, pslot, gqi, slot_free, mxb, nch, wlast, nvalid, vb = stt_
                    W("vector", sve, v2)
                    v3 = E("vector", "tensor_scalar",
                           pslot[:, 0:128 * nvalid], pslot[:, 0:128 * nvalid],
                           cMx[:, par:par + 1], None, op0=ALU.mult, inc=sve)
                    return v3

                def b_tail_part(stt_, v3):
                    hl, qi, par, pslot, gqi, slot_free, mxb, nch, wlast, nvalid, vb = stt_
                    h = 4 * pss + hl
                    qq = qi % 4
                    grp = qi // 4
                    W("tensor", sve, v3)
                    P.ps_pre_write("tensor", (2, 3))
                    ptb = bankb(2, 2)
                    nt_this = min(nvalid, ND)
                    for j in range(nt_this):
                        kw = dict(inc=spe) if j == nt_this - 1 else {}
                        E("tensor", "transpose", ptb[:, 128 * j:128 * (j + 1)],
                          pslot[:, 128 * j:128 * (j + 1)], IDB[:], **kw)
                    probs_T_done[gqi] = spe.n
                    P.ps_post_write((2, 3), spe, spe.n)
                    pT3 = probsT.rearrange("p (k q) -> p k q", k=ND)
                    ptb3 = ptb.rearrange("p (j c) -> p j c", j=ND)
                    nv_ = min(nt_this, 8)
                    pev2 = 0
                    P.ps_pre_read("vector", (2,))
                    pev = E("vector", "tensor_copy",
                            pT3[:, 0:nv_, 128 * qq:128 * (qq + 1)],
                            ptb3[:, 0:nv_, :], inc=sve)
                    P.ps_post_read((2,), sve, pev)
                    if nt_this > 8:
                        P.ps_pre_read("scalar", (3,))
                        pev2 = E("scalar", "activation",
                                 pT3[:, 8:nt_this, 128 * qq:128 * (qq + 1)],
                                 ptb3[:, 8:nt_this, :],
                                 AF.Identity, bias=f32(0.0), scale=f32(1.0), inc=sse)
                        P.ps_post_read((3,), sse, pev2)
                    for kb in range(4 * grp + qq + 1, 4 * grp + 4):
                        E("vector", "memset",
                          pT3[:, kb:kb + 1, 128 * qq:128 * (qq + 1)],
                          0.0, inc=sve)
                    if qq == 3:
                        W("tensor", sve, sve.n)
                        if pev2:
                            W("tensor", sse, pev2)
                        nkb = 4 * grp + 4
                        P.ps_pre_write("tensor", (0,))
                        for k in range(nkb):
                            E("tensor", "matmul", bank(0), vg[:, 128 * k:128 * (k + 1)],
                              probsT[:, 512 * k:512 * (k + 1)],
                              start=(k == 0), stop=(k == nkb - 1),
                              inc=(spe if k == nkb - 1 else None))
                        P.ps_post_write((0,), spe, spe.n)
                        oslot = outTh[grp % 2]
                        W("scalar", sot[grp % 2], outT_dma.get(grp % 2, 0))
                        W("scalar", sl, bias_loaded)
                        P.ps_pre_read("scalar", (0,))
                        oev = E("scalar", "activation", oslot[:], bank(0),
                                AF.Identity, bias=bv_tm[:, g:g + 1],
                                scale=f32(1.0), inc=sse)
                        P.ps_post_read((0,), sse, oev)
                        W("sync", sse, oev)
                        outT_dma[grp % 2] = DMA("sync",
                                                outT_scr[128 * h:128 * (h + 1),
                                                         512 * grp:512 * (grp + 1)],
                                                oslot[:], sot[grp % 2])

                pending = None
                for hl in range(4):
                    for qi in range(16):
                        stt_ = b_head_part(hl, qi)
                        if pending is not None:
                            b_tail_part(*pending)
                        le = b_exp_part(stt_)
                        v2 = b_finish_a(stt_, le)
                        v3 = b_finish_b(stt_, v2)
                        pending = (stt_, v3)
                b_tail_part(*pending)
                A.release()
                A.release()

            barrier()
            A.mark()
            woT = A.b(ND * 2048)
            obuf = [A.b(2048), A.b(2048)]
            bo_row = A.b(2048)
            for d in range(ND):
                DMA("sync", woT[:, 2048 * d:2048 * (d + 1)],
                    wT_scr[128 * d:128 * (d + 1), 3072:5120], swk)
            DMA("gpsimd", bo_row[0:1, :], inp["bo"].unsqueeze(0), sg)
            wo_ld = swk.n
            bo_ld = sg.n
            ob_used = {}
            for t in range(NT):
                sp = t % 2
                W("sync", spe, ob_used.get(sp, 0))
                orb = DMA("sync", obuf[sp].rearrange("p (j c) -> p j c", j=ND),
                          outT_scr[:, 128 * t:128 * (t + 1)].rearrange("(j p) c -> p j c", p=128),
                          sot[sp])
                ybs = (0, 1, 2, 3) if sp == 0 else (4, 5, 6, 7)
                W("tensor", sot[sp], orb)
                W("tensor", swk, wo_ld)
                W("tensor", sg, bo_ld)
                P.ps_pre_write("tensor", ybs)
                for d in range(ND):
                    for cc in range(4):
                        E("tensor", "matmul", bank(ybs[cc]),
                          obuf[sp][:, 128 * d:128 * (d + 1)],
                          woT[:, 2048 * d + 512 * cc:2048 * d + 512 * (cc + 1)],
                          start=(d == 0), stop=False, skip_group_check=True)
                for cc in range(4):
                    E("tensor", "matmul", bank(ybs[cc]), ONEB[0:1, :],
                      bo_row[0:1, 512 * cc:512 * (cc + 1)],
                      start=False, stop=True, skip_group_check=True,
                      inc=(spe if cc == 3 else None))
                ob_used[sp] = spe.n
                P.ps_post_write(ybs, spe, spe.n)
                W("vector", spe, spe.n)
                W("vector", sve, x_add_done[t])
                P.ps_pre_read("vector", ybs)
                xsl = X[:, 2048 * t:2048 * (t + 1)]
                va = E("vector", "tensor_add", xsl, xsl, bank(ybs[0], 4), inc=sve)
                x_add_done[t] = va
                P.ps_post_read(ybs, sve, va)
            A.release()

        if STAGES >= 1:
            emit_attn_a()
        if STAGES >= 2:
            emit_ffn(0)
        if STAGES >= 3:
            emit_attn_b()
        if STAGES >= 4:
            emit_ffn(1)

        for t in range(NT):
            W("sync", sve, x_add_done[t] if x_add_done[t] else sve.n)
            DMA("sync", y_out[128 * t:128 * (t + 1), :], X[:, 2048 * t:2048 * (t + 1)], st)
        W("sync", st, st.n)
        W("sync", sl, sl.n)
        W("sync", sg, sg.n)

        block = ctx.enter_context(nc.Block())
        for eng in Prog.ENGINES:
            fns = P.q[eng]
            getattr(block, eng)(lambda e, fns=fns: [f(e) for f in fns])

    return nc


def make_constants():
    ident = np.eye(128, dtype=np.float32)
    import ml_dtypes
    identb = ident.astype(ml_dtypes.bfloat16)
    i0 = np.arange(64, dtype=np.float64)
    thetas = 10000.0 ** (-2.0 * i0 / 128.0)
    pos = np.arange(1, T + 1, dtype=np.float64)
    ang = pos[None, :] * thetas[:, None]
    cosq = np.concatenate([np.cos(ang), np.cos(ang)], axis=0).astype(np.float32)
    sinq = np.concatenate([-np.sin(ang), np.sin(ang)], axis=0).astype(np.float32)
    q = np.arange(128)[:, None]
    k = np.arange(128)[None, :]
    maskneg = np.where(k > q, -1e9, 0.0).astype(np.float32)
    return dict(identb=identb, identf=ident, cosq=cosq, sinq=sinq, maskneg=maskneg)


def make_in_maps(inputs):
    consts = make_constants()
    x = np.ascontiguousarray(inputs["L0_attn_a_hidden_states"], dtype=np.float32)
    wmap = dict(
        Wq=inputs["L0_attn_b_Wq"], bq=inputs["L0_attn_b_bq"],
        Wk=inputs["L0_attn_b_Wk"], bk=inputs["L0_attn_b_bk"],
        Wv=inputs["L0_attn_b_Wv"], bv=inputs["L0_attn_b_bv"],
        Wo=inputs["L0_attn_b_Wo"], bo=inputs["L0_attn_b_bo"],
        W_router=inputs["L0_ffn_W_router"],
        We_gate=inputs["L0_ffn_We_gate"], We_up=inputs["L0_ffn_We_up"],
        We_down=inputs["L0_ffn_We_down"],
        Ws_gate=inputs["L0_ffn_Ws_gate"], Ws_up=inputs["L0_ffn_Ws_up"],
        Ws_down=inputs["L0_ffn_Ws_down"],
    )
    for p in "qkvo":
        wmap[f"basis_{p}"] = inputs[f"L0_attn_a_basis_{p}"]
        wmap[f"phase_{p}"] = inputs[f"L0_attn_a_phase_{p}"]
        wmap[f"amp_{p}"] = inputs[f"L0_attn_a_amp_{p}"]
    wmap = {k: np.ascontiguousarray(v, dtype=np.float32) for k, v in wmap.items()}
    return [dict(x=np.ascontiguousarray(x[c]), **wmap, **consts)
            for c in range(x.shape[0])]


def kernel(**inputs):
    nc = build_nc()
    in_maps = make_in_maps(inputs)
    res = run_bass_kernel_spmd(nc, in_maps, core_ids=list(range(8)))
    return np.stack([np.asarray(r["y"], dtype=np.float32) for r in res.results], axis=0)
